# version 12
# baseline (speedup 1.0000x reference)
import numpy as np
import concourse.bass as bass
import concourse.mybir as mybir
from concourse.bass_utils import run_bass_kernel_spmd

F32 = mybir.dt.float32
BF16 = mybir.dt.bfloat16
AF = mybir.ActivationFunctionType
OP = mybir.AluOpType
AX = mybir.AxisListType

NCORES = 8
T = 2048
D = 1024
CH = 8
NGS = [(i * 512, 512) for i in range(4)]
EPS = 1e-5
NEG = -3.0e38

XB = 0
BB = 16384
D2B = 32832
FWA, FWB, FWC = 16384, 24576, 32768
FH = 41024
FMB = 45120
FMB2 = 51264
FMF = 47168
FTM = 54336
SLC = 49216
BC0 = 50240
BC1 = 52288
MS0 = 53312
MS1 = 54336
MS2 = 55360
X2A = 55360
X2B = 55872
ARB_COLS = 56384

TR_A, TR_B, TR_H, TR_C = 0, 1024, 2048, 3072
TR_ROWS = 4096
R_MU, R_T0, R_T1, R_R, R_MR = (TR_ROWS + i * 512 for i in range(5))
TR_A2, TR_B2, TR_C2 = 4096, 5120, 6656
TRANS_COLS = 7680

PHASE_LIMIT = 99
DBG = None


class Prog:
    ENGS = ("sync", "tensor", "vector", "scalar", "gpsimd")

    def __init__(self):
        self.st = {e: [] for e in self.ENGS}
        self.cnt = {}
        self.sems = {}

    def emit(self, eng, fn):
        self.st[eng].append(fn)

    def wait(self, eng, mark):
        if mark is None:
            return
        sem, val = mark
        if val <= 0:
            return
        self.emit(eng, lambda e, s=sem, v=val: e.wait_ge(self.sems[s], v))

    def cur(self, sem):
        return (sem, self.cnt.get(sem, 0))

    def op(self, eng, fn, inc=None, n=1, nd=False):
        if eng in ("vector", "scalar") and not nd:
            self.emit(eng, lambda e: e.drain())
        if inc is None:
            self.emit(eng, fn)
            return None
        self.cnt[inc] = self.cnt.get(inc, 0) + n
        tgt = self.cnt[inc]

        def g(e, fn=fn, s=inc, k=n):
            fn(e).then_inc(self.sems[s], k)
        self.emit(eng, g)
        return (inc, tgt)


class Banks:
    def __init__(self, prog, aps):
        self.p = prog
        self.aps = list(aps)
        self.marks = [None] * len(self.aps)
        self.i = 0

    def get(self):
        i = self.i % len(self.aps)
        self.i += 1
        return i, self.aps[i], self.marks[i]

    def set_mark(self, i, mark):
        self.marks[i] = mark


SMALLS = {
    "binA": ([128, 16], F32), "lng": ([128, 8], F32), "lnb": ([128, 8], F32),
    "wcA": ([128, 32], F32), "bcA": ([128, 8], F32), "bdA": ([128, 8], F32),
    "bBA": ([128, 8], F32), "bCA": ([128, 8], F32), "boA": ([128, 8], F32),
    "wcB": ([128, 32], F32), "bcB": ([128, 8], F32), "bgB": ([128, 16], F32),
    "fb": ([128, 8], F32), "rT": ([128, 128], BF16), "rb": ([16, 1], F32),
    "fb1": ([128, 32], F32), "fb2": ([128, 8], F32),
    "eye16": ([16, 16], F32), "eye128f": ([128, 128], F32),
    "eye128b": ([128, 128], BF16),
    "selAH": ([16, 16], F32), "hsel": ([24, 3], F32), "spre": ([128, 8], F32),
}
LATE = {"oh16": ([16, 2048], BF16)}

SCRATCH = {
    "nbdA": ([128, 8], F32), "sp8": ([128, 8], F32), "sptmp": ([128, 8], F32),
    "ones_row": ([1, 128], BF16), "ones_col": ([128, 1], BF16),
    "sum24": ([128, 24], F32), "tr24": ([24, 128], F32),
    "selev": ([128, 16], F32), "hcar": ([128, 8], F32), "dcar": ([128, 8], F32),
    "hp": ([128, 8], F32), "hpb": ([128, 8], BF16),
    "mx": ([128, 16], F32),
    "cmb": ([128, 24], F32),
    "sv24": ([128, 24], F32), "cv6": ([128, 8], F32), "cv3": ([128, 4], F32),
}


def build_nc(phase_limit):
    nc = bass.Bass("TRN2", target_bir_lowering=False)
    p = Prog()

    def din(name, shape, dt=F32):
        return nc.dram_tensor(name, shape, dt, kind="ExternalInput").ap()

    x_in = din("x_fm", [D, T])
    out_d = nc.dram_tensor("out_fm", [D, T], F32, kind="ExternalOutput").ap()

    wA_inT = din("wA_inT", [D, 2 * D])
    wA_dT = din("wA_dT", [D, D])
    wA_BT = din("wA_BT", [D, D])
    wA_CT = din("wA_CT", [D, D])
    wA_outT = din("wA_outT", [D, D])
    wB_inT = din("wB_inT", [D, 2 * D])
    wB_gT = din("wB_gT", [D, 2 * D])
    wB_outT = din("wB_outT", [D, D])
    f_w1T = din("f_w1T", [D, 4 * D])
    f_w2T = din("f_w2T", [4 * D, D])
    din_small = {k: din(k, s, d) for k, (s, d) in SMALLS.items()}
    din_oh16 = din("oh16", [16, 2048], BF16)

    agh_in = [nc.dram_tensor(f"agh{i}_in", [3, D], F32) for i in range(2)]
    agh_out = [nc.dram_tensor(f"agh{i}_out", [24, D], F32) for i in range(2)]
    ags_in = [nc.dram_tensor(f"ags{i}_in", [2, D], F32) for i in range(2)]
    ags_out = [nc.dram_tensor(f"ags{i}_out", [16, D], F32) for i in range(2)]

    RES = nc.alloc_sbuf_tensor("RES", [128, 16384], F32).ap()
    TRANS = nc.alloc_sbuf_tensor("TRANS", [128, TRANS_COLS], F32).ap()
    ARB = nc.alloc_sbuf_tensor("ARB", [128, ARB_COLS], BF16).ap()

    t = {k: nc.alloc_sbuf_tensor("t_" + k, s, d).ap() for k, (s, d) in SMALLS.items()}
    t["oh16"] = ARB[0:16, SLC:SLC + 2048]
    for k, (s, d) in SCRATCH.items():
        t[k] = nc.alloc_sbuf_tensor("t_" + k, s, d).ap()
    t["agh2"] = TRANS[0:24, 4096:5120]
    t["ags"] = TRANS[0:16, 0:1024]
    t["ysrow"] = TRANS[0:1, 4096:6144]
    t["yscor"] = TRANS[0:1, 1024:3072]
    t["ysbf"] = ARB[0:1, BC1:BC1 + 2048]
    t["rowseg"] = ARB[0:1, FMF:FMF + 1024]
    t["ltm"] = TRANS[:, 2048:2304]
    t["ltm2"] = TRANS[:, 2304:2560]

    PF = [nc.alloc_psum_tensor(f"pf{i}", [128, 512], F32).ap() for i in range(7)]
    PB7 = nc.alloc_psum_tensor("pb7", [128, 1024], BF16).ap()

    def res_c(cc, c0=0, w=T):
        return RES[:, cc * T + c0: cc * T + c0 + w]

    def arb(base, c0, w):
        return ARB[:, base + c0: base + c0 + w]

    def xnc(cc, c0=0, w=T):
        return arb(XB, cc * T + c0, w)

    def ubuf(cc, c0, w):
        return arb(BB, cc * 2056 + 3 + c0, w)

    def ubuf_raw(cc, c0, w):
        return arb(BB, cc * 2056 + c0, w)

    def dma(eng, out, in_, sem="s_x", noncontig=False, **kw):
        if noncontig:
            def fn(e, o=out, i=in_, kw=kw):
                with nc.allow_non_contiguous_dma("tiny"):
                    return e.dma_start(out=o, in_=i, **kw)
        else:
            def fn(e, o=out, i=in_, kw=kw):
                return e.dma_start(out=o, in_=i, **kw)
        return p.op(eng, fn, inc=sem, n=16)

    def mm(o, l, r, s, tt_):
        return lambda e, o=o, l=l, r=r, s=s, tt_=tt_: e.matmul(o, l, r, start=s, stop=tt_)

    def vcopy(out, in_):
        return lambda e, o=out, i=in_: e.tensor_copy(o, i)

    def act(out, in_, func, bias=None, scale=None):
        kw = {}
        if bias is not None:
            kw["bias"] = bias
        if scale is not None:
            kw["scale"] = scale
        return lambda e, o=out, i=in_, f=func, kw=kw: e.activation(out=o, in_=i, func=f, **kw)

    def tt(out, a, b, op):
        return lambda e, o=out, a=a, b=b, op=op: e.tensor_tensor(o, a, b, op)

    def ts2(out, a, s1, s2, op0, op1):
        return lambda e, o=out, a=a, s1=s1, s2=s2, op0=op0, op1=op1: \
            e.tensor_scalar(o, a, s1, s2, op0, op1)

    def ts1(out, a, s1, op0):
        return lambda e, o=out, a=a, s1=s1, op0=op0: e.tensor_scalar(o, a, s1, None, op0)

    def stt(out, in0, sc, in1, op0, op1):
        return lambda e, o=out, i0=in0, s=sc, i1=in1, op0=op0, op1=op1: \
            e.scalar_tensor_tensor(o, i0, s, i1, op0, op1)

    def scan(out, d0, d1, init, op0, op1):
        return lambda e, o=out, a=d0, b=d1, i=init, op0=op0, op1=op1: \
            e.tensor_tensor_scan(o, a, b, i, op0, op1)

    gemm_banks = Banks(p, [PF[0], PF[1], PF[2], PF[3]])

    def emit_gemm(K, JOs, lhsT_fn, rhs_fn, evac_fn, ngs=NGS, extraK=None, banks=None):
        banks = banks or gemm_banks
        for jo in JOs:
            for (c0, w) in ngs:
                bi, ps, bmark = banks.get()
                p.wait("tensor", bmark)
                mmark = None
                for k in range(K):
                    last = (k == K - 1) and extraK is None
                    f = mm(ps[:, 0:w], lhsT_fn(k, jo), rhs_fn(k, c0, w), k == 0, last)
                    if last:
                        mmark = p.op("tensor", f, inc="s_mm")
                    else:
                        p.op("tensor", f)
                if extraK is not None:
                    mmark = extraK(jo, c0, w, ps)
                emark = evac_fn(jo, c0, w, ps[:, 0:w], mmark)
                banks.set_mark(bi, emark)

    def ln_phase(kind):
        cast_m = []
        for cc in range(CH):
            cm = p.op("vector", vcopy(xnc(cc), res_c(cc)), inc="s_v", nd=cc > 0)
            cast_m.append(cm)
        bcb = Banks(p, [PF[0], PF[1], PF[2]])
        apply_deps = []
        x2segs = [ARB[:, X2A:X2A + 512], ARB[:, X2B:X2B + 512]]
        x2_free = [None, None]
        sumx_bank, sumsq_bank = PF[5], PF[6]
        sx_free = None
        sq_free = None
        for gi, (c0, w) in enumerate(NGS):
            p.wait("tensor", sx_free)
            sxm = None
            for cc in range(CH):
                if gi == 0:
                    p.wait("tensor", cast_m[cc])
                f = mm(sumx_bank[0:1, 0:w], t["ones_col"], xnc(cc, c0, w),
                       cc == 0, cc == CH - 1)
                if cc == CH - 1:
                    sxm = p.op("tensor", f, inc="s_mm")
                else:
                    p.op("tensor", f)
            p.wait("tensor", sq_free)
            sqgm = None
            for cc in range(CH):
                seg = x2segs[cc % 2]
                p.wait("scalar", cast_m[cc])
                p.wait("scalar", x2_free[cc % 2])
                sqm = p.op("scalar", act(seg, xnc(cc, c0, w), AF.Square),
                           inc="s_sc", nd=not (gi == 0 and cc == 0))
                p.wait("tensor", sqm)
                sqgm = p.op("tensor", mm(sumsq_bank[0:1, 0:w], t["ones_col"], seg,
                                         cc == 0, cc == CH - 1), inc="s_mm")
                x2_free[cc % 2] = sqgm
            def ROW(off, w=w):
                return TRANS[0:1, off:off + w]
            p.wait("vector", sxm)
            sx_free = p.op("vector", ts1(ROW(R_MU), sumx_bank[0:1, 0:w],
                                         1.0 / D, OP.mult), inc="s_v")
            p.op("vector", tt(ROW(R_T0), ROW(R_MU), ROW(R_MU), OP.mult))
            p.wait("vector", sqgm)
            p.op("vector", stt(ROW(R_T0), sumsq_bank[0:1, 0:w], 1.0 / D, ROW(R_T0),
                               OP.mult, OP.subtract))
            vm = p.op("vector", ts1(ROW(R_T0), ROW(R_T0), EPS, OP.add), inc="s_v")
            sq_free = vm
            p.wait("scalar", vm)
            p.op("scalar", act(ROW(R_T1), ROW(R_T0), AF.Ln))
            if kind == "plain":
                smk = p.op("scalar", act(ROW(R_R), ROW(R_T1), AF.Exp, scale=-0.5),
                           inc="s_sc")
                p.wait("vector", smk)
                p.op("vector", tt(ROW(R_MR), ROW(R_MU), ROW(R_R), OP.mult))
            else:
                r1m = p.op("scalar", act(ROW(R_R), ROW(R_T1), AF.Exp, scale=-0.5),
                           inc="s_sc")
                p.wait("vector", r1m)
                p.wait("vector", p.cur("s_mm"))
                sgm = p.op("vector", vcopy(t["rowseg"][0:1, 0:w], ROW(R_R)), inc="s_v")
                bi, ps, bmk = bcb.get()
                p.wait("tensor", bmk)
                p.wait("tensor", sgm)
                bmm = p.op("tensor", mm(ps[:, 0:w], t["ones_row"],
                                        t["rowseg"][0:1, 0:w], True, True), inc="s_mm")
                p.wait("vector", bmm)
                r1bc = p.op("vector", vcopy(arb(BB, c0, w), ps[:, 0:w]), inc="s_v")
                bcb.set_mark(bi, r1bc)
                apply_deps.append(r1bc)
                p.op("vector", tt(ROW(R_MR), ROW(R_MU), ROW(R_R), OP.mult))
                p.wait("vector", p.cur("s_mm"))
                sgm2 = p.op("vector", vcopy(t["rowseg"][0:1, 0:w], ROW(R_MR)),
                            inc="s_v")
                bi2, ps2, bmk2 = bcb.get()
                p.wait("tensor", bmk2)
                p.wait("tensor", sgm2)
                bmm2 = p.op("tensor", mm(ps2[:, 0:w], t["ones_row"],
                                         t["rowseg"][0:1, 0:w], True, True),
                            inc="s_mm")
                p.wait("vector", bmm2)
                m1bc = p.op("vector", vcopy(arb(BB, 2048 + c0, w), ps2[:, 0:w]),
                            inc="s_v")
                bcb.set_mark(bi2, m1bc)
                apply_deps.append(m1bc)
                sm2 = p.op("scalar", act(ROW(R_T0), ROW(R_T1), AF.Exp, scale=-1.0),
                           inc="s_sc")
                p.wait("vector", sm2)
                vm2 = p.op("vector", ts2(ROW(R_T0), ROW(R_T0), -EPS, 1.0 + EPS,
                                         OP.mult, OP.add), inc="s_v")
                p.wait("scalar", vm2)
                lm = p.op("scalar", act(ROW(R_T0), ROW(R_T0), AF.Ln), inc="s_sc")
                p.wait("vector", lm)
                vm3 = p.op("vector", tt(ROW(R_T0), ROW(R_T1), ROW(R_T0), OP.add),
                           inc="s_v")
                p.wait("scalar", vm3)
                smk = p.op("scalar", act(ROW(R_R), ROW(R_T0), AF.Exp, scale=-0.5),
                           inc="s_sc")
                p.wait("vector", smk)
                p.op("vector", tt(ROW(R_MR), ROW(R_MU), ROW(R_R), OP.mult))
            p.wait("vector", p.cur("s_mm"))
            c1 = p.op("vector", vcopy(t["rowseg"][0:1, 0:w], ROW(R_R)), inc="s_v")
            c2 = p.op("vector", vcopy(t["rowseg"][0:1, 512:512 + w], ROW(R_MR)),
                      inc="s_v")
            for (src_off, dst_base, dep) in ((0, BC0, c1), (512, BC1, c2)):
                bi, ps, bmk = bcb.get()
                p.wait("tensor", bmk)
                p.wait("tensor", dep)
                bmmk = p.op("tensor", mm(ps[:, 0:w], t["ones_row"],
                                         t["rowseg"][0:1, src_off:src_off + w],
                                         True, True), inc="s_mm")
                p.wait("vector", bmmk)
                cpm = p.op("vector", vcopy(arb(dst_base, c0, w), ps[:, 0:w]),
                           inc="s_v")
                bcb.set_mark(bi, cpm)
                apply_deps.append(cpm)
        stats_done = p.cur("s_mm")
        for mk in apply_deps:
            p.wait("vector", mk)
        p.wait("vector", stats_done)
        apply_m = []
        for cc in range(CH):
            if kind == "double":
                tmpb = arb(BB, 4096, T)
                p.op("vector", tt(tmpb, xnc(cc), arb(BB, 0, T), OP.mult),
                     nd=cc > 0)
                p.op("vector", tt(tmpb, tmpb, arb(BB, 2048, T), OP.subtract))
                p.op("vector", tt(res_c(cc), res_c(cc), tmpb, OP.add))
            p.op("vector", tt(xnc(cc), xnc(cc), arb(BC0, 0, T), OP.mult),
                 nd=(cc > 0 and kind != "double"))
            am = p.op("vector", tt(xnc(cc), xnc(cc), arb(BC1, 0, T), OP.subtract),
                      inc="s_v")
            if kind == "double":
                am = p.op("vector", lambda e, o=xnc(cc), cc=cc:
                          e.tensor_scalar(o, o, t["lng"][:, cc:cc + 1],
                                          t["lnb"][:, cc:cc + 1], OP.mult, OP.add),
                          inc="s_v")
            apply_m.append(am)
        return apply_m

    def emit_ag(idx, nrows, src_mark, halo):
        dram_in = (agh_in if halo else ags_in)[idx]
        dram_out = (agh_out if halo else ags_out)[idx]
        ncols = 8 * nrows
        p.wait("tensor", src_mark)
        p.wait("tensor", p.cur("s_sc"))
        tm = p.op("tensor", lambda e, n=ncols: e.transpose(
            PF[6][0:n, 0:128], t["sum24"][:, 0:n], t["eye128f"]), inc="s_mm")
        p.wait("vector", tm)
        vm = p.op("vector", vcopy(t["tr24"][0:ncols, :], PF[6][0:ncols, 0:128]),
                  inc="s_v")
        p.wait("gpsimd", vm)
        dm = dma("gpsimd", dram_in.ap().rearrange("r (c q) -> (r c) q", q=128),
                 t["tr24"][0:ncols, :], sem="s_g")
        p.wait("gpsimd", dm)
        ccm = p.op("gpsimd", lambda e, i=dram_in, o=dram_out: e.collective_compute(
            "AllGather", OP.bypass, replica_groups=[list(range(NCORES))],
            ins=[i.ap().opt()], outs=[o.ap().opt()]), inc="s_cc")
        dst = t["agh2"] if halo else t["ags"]
        p.wait("sync", ccm)
        rb = dma("sync", dst, dram_out.ap(), sem="s_ag")
        return rb

    def emit_tails_chunk(cc, evac_mark):
        p.wait("vector", evac_mark)
        mk = None
        for r in range(3):
            base = BB + cc * 2056
            p.op("vector", vcopy(t["sum24"][:, r * 8 + cc:r * 8 + cc + 1],
                                 ARB[:, base + 2048 + r:base + 2048 + r + 1]),
                 nd=True)
            mk = p.op("vector", vcopy(t["sv24"][:, r * 8 + cc:r * 8 + cc + 1],
                                      ARB[:, base + 3 + r:base + 3 + r + 1]),
                      inc="s_v", nd=True)
        return mk

    def emit_conv_chunk(cc, wc, bc, evac_mark, par, prev_cm):
        accw = TRANS[:, par * 2048:par * 2048 + 2048]
        p.wait("vector", evac_mark)
        p.wait("vector", prev_cm)
        p.op("vector", lambda e, o=accw, i=ubuf_raw(cc, 0, T),
             s=wc[:, cc * 4:cc * 4 + 1]: e.tensor_scalar(o, i, s, None, OP.mult))
        vm = None
        for k in range(1, 4):
            f = lambda e, o=accw, i=ubuf_raw(cc, k, T), \
                s=wc[:, cc * 4 + k:cc * 4 + k + 1]: \
                e.scalar_tensor_tensor(o, i, s, o, OP.mult, OP.add)
            if k == 3:
                vm = p.op("vector", f, inc="s_v")
            else:
                p.op("vector", f)
        p.wait("scalar", vm)
        cm = p.op("scalar", act(ubuf(cc, 0, T), accw, AF.Identity,
                                bias=bc[:, cc:cc + 1]), inc="s_sc", nd=True)
        return cm

    def emit_conv_fixup(rb_mark, wc, bc):
        p.wait("tensor", rb_mark)
        marks = []
        pm = None
        for cc in range(CH):
            p.wait("tensor", pm)
            tmm = p.op("tensor", mm(PF[6][:, 0:3],
                                    t["agh2"][0:24, cc * 128:(cc + 1) * 128],
                                    t["hsel"], True, True), inc="s_mm")
            p.wait("vector", tmm)
            p.op("vector", vcopy(t["cv6"][:, 0:3], PF[6][:, 0:3]))
            for r in range(3):
                p.op("vector", vcopy(t["cv6"][:, 3 + r:4 + r],
                                     t["sv24"][:, r * 8 + cc:r * 8 + cc + 1]),
                     nd=True)
            p.op("vector", lambda e, cc=cc: e.tensor_scalar(
                t["cv3"][:, 0:3], t["cv6"][:, 0:3], wc[:, cc * 4:cc * 4 + 1],
                None, OP.mult))
            vm = None
            for k in range(1, 4):
                f = lambda e, k=k, cc=cc: e.scalar_tensor_tensor(
                    t["cv3"][:, 0:3], t["cv6"][:, k:k + 3],
                    wc[:, cc * 4 + k:cc * 4 + k + 1], t["cv3"][:, 0:3],
                    OP.mult, OP.add)
                if k == 3:
                    vm = p.op("vector", f, inc="s_v")
                else:
                    p.op("vector", f)
            pm = vm
            p.wait("scalar", vm)
            cm = p.op("scalar", act(ubuf(cc, 0, 3), t["cv3"][:, 0:3], AF.Identity,
                                    bias=bc[:, cc:cc + 1]), inc="s_sc")
            marks.append(cm)
        return marks

    def emit_combine(rb_mark):
        p.wait("tensor", rb_mark)
        marks = []
        pm = None
        for cc in range(CH):
            p.wait("tensor", pm)
            tmm = p.op("tensor", mm(PF[6][:, 0:16],
                                    t["ags"][0:16, cc * 128:(cc + 1) * 128],
                                    t["selAH"], True, True), inc="s_mm")
            p.wait("vector", tmm)
            pm = p.op("vector", vcopy(t["selev"], PF[6][:, 0:16]), inc="s_v")
            TA, TH, TS = t["cmb"][:, 0:8], t["cmb"][:, 8:16], t["cmb"][:, 16:24]
            p.op("vector", ts1(TA, t["selev"][:, 0:8], -1.0, OP.add))
            p.op("vector", tt(TA, TA, t["spre"], OP.mult))
            p.op("vector", ts1(TA, TA, 1.0, OP.add))
            p.op("vector", tt(TH, t["selev"][:, 8:16], t["spre"], OP.mult))
            p.op("vector", scan(TS, TA, TH, 0.0, OP.mult, OP.add))
            cm = p.op("vector", vcopy(t["hp"][:, cc:cc + 1], TS[:, 7:8]), inc="s_v")
            marks.append(cm)
        return marks

    dma("sync", RES.rearrange("p (c q) -> p c q", q=T),
        x_in.rearrange("(c p) q -> p c q", p=128), sem="s_x")
    for k in din_small:
        dma("sync", t[k], din_small[k], sem="s_x")
    ld0 = p.cur("s_x")
    p.wait("vector", ld0)
    p.op("vector", lambda e: e.memset(t["ones_row"], 1.0))
    p.op("vector", lambda e: e.memset(t["ones_col"], 1.0))
    for cc in range(CH):
        p.op("vector", lambda e, a=ARB[:, BB + cc * 2056:BB + cc * 2056 + 3]:
             e.memset(a, 0.0), nd=True)
    p.op("vector", ts1(t["nbdA"], t["bdA"], -1.0, OP.mult))
    rl = p.op("vector", ts1(t["sp8"], t["fb"], 0.0, OP.max), inc="s_v")
    p.wait("scalar", rl)
    p.op("scalar", act(t["sptmp"], t["fb"], AF.Abs))
    p.op("scalar", act(t["sptmp"], t["sptmp"], AF.Exp, scale=-1.0))
    lm0 = p.op("scalar", act(t["sptmp"], t["sptmp"], AF.Ln, bias=1.0), inc="s_sc")
    p.wait("vector", lm0)
    p.op("vector", tt(t["sp8"], t["sp8"], t["sptmp"], OP.add))
    p.op("vector", ts1(t["sp8"], t["sp8"], -8.0, OP.mult))
    WINU = D2B
    for cc in range(CH):
        dma("gpsimd", arb(WINU, cc * 1024, 1024),
            wA_inT[cc * 128:(cc + 1) * 128, 0:1024], sem="s_w")
    wu_mark = p.cur("s_w")

    def dump_bf(base, stride, off):
        p.wait("gpsimd", p.cur("s_v"))
        p.wait("gpsimd", p.cur("s_sc"))
        for cc in range(CH):
            dma("gpsimd", out_d[cc * 128:(cc + 1) * 128, :],
                ARB[:, base + cc * stride + off: base + cc * stride + off + T],
                sem="s_st")
        p.wait("sync", p.cur("s_st"))

    def dump_f32row(ap_row):
        p.wait("sync", p.cur("s_v"))
        dma("sync", out_d[0:1, :], ap_row, sem="s_st")
        p.wait("sync", p.cur("s_st"))

    def attn_a():
        apply_m = ln_phase("double")
        if DBG in ("bc0", "bc1"):
            dump_bf(BC0 if DBG == "bc0" else BC1, 0, 0)
            return None
        if DBG == "xn":
            for mk in apply_m:
                p.wait("gpsimd", mk)
            dump_bf(XB, T, 0)
            return None
        for mk in apply_m:
            p.wait("tensor", mk)
        p.wait("tensor", wu_mark)

        evu_n = [0]

        def ev_u(jo, c0, w, ps, mmark):
            p.wait("scalar", mmark)
            evu_n[0] += 1
            return p.op("scalar", act(ubuf(jo, c0, w), ps, AF.Identity,
                                      bias=t["binA"][:, jo:jo + 1]), inc="s_sc",
                        nd=evu_n[0] > 1)
        prev_cm = [None, None]
        tails_mk = None
        for jo in range(8):
            emit_gemm(8, [jo],
                      lambda k, jjo: arb(WINU, k * 1024 + jjo * 128, 128),
                      lambda k, c0, w: xnc(k, c0, w), ev_u)
            evm = p.cur("s_sc")
            if DBG == "u":
                continue
            tails_mk = emit_tails_chunk(jo, evm)
            prev_cm[jo % 2] = emit_conv_chunk(jo, t["wcA"], t["bcA"], evm,
                                              jo % 2, prev_cm[jo % 2])
        if DBG == "u":
            dump_bf(BB, 2056, 3)
            return None
        rb1 = emit_ag(0, 3, tails_mk, True)
        halo_m = emit_conv_fixup(rb1, t["wcA"], t["bcA"])
        conv_m = [p.cur("s_sc")]
        if DBG == "uc":
            for cm in conv_m:
                p.wait("gpsimd", cm)
            dump_bf(BB, 2056, 3)
            return None

        zbanks = Banks(p, [PF[0], PF[1], PF[6]])
        zbanks.marks[2] = halo_m[-1]
        ys_banks = [PF[2], PF[3], PF[4], PF[5]]
        ys_marks = [None] * 4
        prev_v_par = [None, None]
        A_P = (TR_A, TR_A2)
        B_P = (TR_B, TR_B2)
        C_P = (MS1, MS2)
        MS0_P = (MS0, SLC + 3072)
        ys_guard = [None, None]
        pend = [None]

        def flush_ys():
            if pend[0] is None:
                return
            pcc, phalf, ppar, phcm = pend[0]
            pend[0] = None
            for sub in range(2):
                gidx = phalf * 2 + sub
                p.wait("tensor", phcm)
                f = mm(ys_banks[gidx][0:1, 0:512], t["ones_col"],
                       arb(MS0_P[ppar], sub * 512, 512), pcc == 0, pcc == CH - 1)
                m = p.op("tensor", f, inc="s_mm")
                ys_guard[ppar] = m
                if pcc == CH - 1:
                    ys_marks[gidx] = m
        DBCR = [SLC + i * 1024 for i in range(3)]
        dbc_ring_guard = [None] * 3
        dbc_slice_m = [None] * 24
        ln_apply_v = p.cur("s_v")
        dbc_srcs = (wA_dT, wA_BT, wA_CT)

        def load_dbc(s):
            cc2, pi2 = s // 3, s % 3
            ri = s % 3
            p.wait("gpsimd", dbc_ring_guard[ri])
            if s < 3:
                p.wait("gpsimd", ln_apply_v)
            dma("gpsimd", arb(DBCR[ri], 0, 1024).rearrange(
                "p (c q) -> p c q", q=128),
                dbc_srcs[pi2][:, cc2 * 128:(cc2 + 1) * 128].rearrange(
                    "(c p) q -> p c q", p=128), sem="s_w")
            dbc_slice_m[s] = p.cur("s_w")
        for s in range(3):
            load_dbc(s)
        for cc in range(CH):
            if cc == 0:
                for cm in conv_m:
                    p.wait("tensor", cm)
            for half in range(2):
                hs = half * 1024
                par = (cc * 2 + half) % 2
                TRA = TRANS[:, A_P[par]:A_P[par] + 1024]
                TRB = TRANS[:, B_P[par]:B_P[par] + 1024]
                CSL = arb(C_P[par], 0, 1024)
                evC = None
                first_ev = True
                for pi in range(3):
                    s = cc * 3 + pi
                    if half == 0:
                        p.wait("tensor", dbc_slice_m[s])
                    for sub in range(2):
                        c0 = hs + sub * 512
                        bi, ps, bmk = zbanks.get()
                        p.wait("tensor", bmk)
                        mmark = None
                        for k in range(CH):
                            f = mm(ps[:, 0:512],
                                   arb(DBCR[s % 3], k * 128, 128),
                                   ubuf(k, c0, 512), k == 0, k == CH - 1)
                            if k == CH - 1:
                                mmark = p.op("tensor", f, inc="s_mm")
                            else:
                                p.op("tensor", f)
                        p.wait("scalar", mmark)
                        p.wait("scalar", prev_v_par[par])
                        off = sub * 512
                        if pi == 0:
                            emk = p.op("scalar", act(
                                TRA[:, off:off + 512], ps[:, 0:512],
                                AF.Sigmoid, bias=t["nbdA"][:, cc:cc + 1], scale=-1.0),
                                inc="s_sc", nd=not first_ev)
                        elif pi == 1:
                            emk = p.op("scalar", act(
                                TRB[:, off:off + 512], ps[:, 0:512],
                                AF.Identity, bias=t["bBA"][:, cc:cc + 1]),
                                inc="s_sc", nd=True)
                        else:
                            emk = p.op("scalar", act(
                                CSL[:, off:off + 512], ps[:, 0:512],
                                AF.Identity, bias=t["bCA"][:, cc:cc + 1]),
                                inc="s_sc", nd=True)
                            evC = emk
                        first_ev = False
                        zbanks.set_mark(bi, emk)
                    if half == 1:
                        dbc_ring_guard[s % 3] = p.cur("s_mm")
                        if s + 3 < 24:
                            load_dbc(s + 3)
                flush_ys()
                MS0B = arb(MS0_P[par], 0, 1024)
                p.wait("vector", evC)
                p.wait("vector", ys_guard[par])
                p.op("vector", tt(MS0B, ubuf(cc, hs, 1024),
                                  TRB, OP.mult))
                init_h = 0.0 if half == 0 else t["hcar"][:, cc:cc + 1]
                init_d = 1.0 if half == 0 else t["dcar"][:, cc:cc + 1]
                p.op("vector", scan(TRANS[:, TR_H:TR_H + 1024], TRA,
                                    MS0B, init_h, OP.mult, OP.add))
                p.op("vector", scan(TRANS[:, TR_C:TR_C + 1024], TRA, TRA, init_d,
                                    OP.mult, OP.bypass), nd=True)
                hcm = p.op("vector", tt(MS0B,
                                        TRANS[:, TR_H:TR_H + 1024],
                                        CSL, OP.mult), inc="s_v")
                pend[0] = (cc, half, par, hcm)
                p.op("vector", tt(arb(D2B, cc * 2048 + hs, 1024),
                                  TRANS[:, TR_C:TR_C + 1024],
                                  CSL, OP.mult), nd=True)
                p.op("vector", vcopy(t["hcar"][:, cc:cc + 1],
                                     TRANS[:, TR_H + 1023:TR_H + 1024]), nd=True)
                prev_v_par[par] = p.op("vector", vcopy(
                    t["dcar"][:, cc:cc + 1], TRANS[:, TR_C + 1023:TR_C + 1024]),
                    inc="s_v", nd=True)
        flush_ys()
        d2_all = p.cur("s_v")
        if DBG == "d2":
            dump_bf(D2B, 2048, 0)
            return None
        ysev = []
        for gidx in range(4):
            p.wait("vector", ys_marks[gidx])
            ev = p.op("vector", vcopy(t["ysrow"][0:1, gidx * 512:(gidx + 1) * 512],
                                      ys_banks[gidx][0:1, 0:512]), inc="s_v")
            ysev.append(ev)
        if DBG == "ysl":
            dump_f32row(t["ysrow"])
            return None
        p.op("vector", vcopy(t["sum24"][:, 0:8], t["dcar"]))
        sm = p.op("vector", vcopy(t["sum24"][:, 8:16], t["hcar"]), inc="s_v")
        rb2 = emit_ag(0, 2, sm, False)
        if DBG == "ags":
            p.wait("sync", rb2)
            p.wait("sync", p.cur("s_v"))
            stm_ = dma("sync", out_d[0:16, 0:1024], t["ags"], sem="s_st")
            stm_ = dma("sync", out_d[16:144, 1024:1048], t["sum24"], sem="s_st")
            p.wait("sync", stm_)
            return None
        cmb_m = emit_combine(rb2)
        for mk in cmb_m:
            p.wait("vector", mk)
        hpb_m = p.op("vector", vcopy(t["hpb"], t["hp"]), inc="s_v")
        if DBG == "hp":
            p.wait("sync", p.cur("s_v"))
            dma("sync", out_d[0:128, 0:8], t["hp"], sem="s_st")
            dma("sync", out_d[0:128, 8:16], t["spre"], sem="s_st")
            dma("sync", out_d[0:128, 16:40], t["cmb"], sem="s_st")
            stm_ = dma("sync", out_d[0:128, 40:56], t["selev"], sem="s_st")
            p.wait("sync", stm_)
            return None
        p.wait("tensor", hpb_m)
        p.wait("tensor", d2_all)
        yc = Banks(p, [PF[0], PF[1]])
        for gi, (c0, w) in enumerate(NGS):
            bi, ps, bmk = yc.get()
            p.wait("tensor", bmk)
            mmark = None
            for cc in range(CH):
                f = mm(ps[0:1, 0:w], t["hpb"][:, cc:cc + 1],
                       arb(D2B, cc * 2048 + c0, w), cc == 0, cc == CH - 1)
                if cc == CH - 1:
                    mmark = p.op("tensor", f, inc="s_mm")
                else:
                    p.op("tensor", f)
            p.wait("vector", mmark)
            ev = p.op("vector", vcopy(t["yscor"][0:1, c0:c0 + w], ps[0:1, 0:w]),
                      inc="s_v")
            yc.set_mark(bi, ev)
        for mk in ysev:
            p.wait("vector", mk)
        p.op("vector", tt(t["ysrow"], t["ysrow"], t["yscor"], OP.add))
        ysb = p.op("vector", vcopy(t["ysbf"], t["ysrow"]), inc="s_v")
        if DBG == "ys":
            dump_f32row(t["ysrow"])
            return None
        ysbc = Banks(p, [PF[0], PF[1]])
        ysbc_m = []
        for gi, (c0, w) in enumerate(NGS):
            bi, ps, bmk = ysbc.get()
            p.wait("tensor", bmk)
            p.wait("tensor", ysb)
            bmmk = p.op("tensor", mm(ps[:, 0:w], t["ones_row"],
                                     t["ysbf"][0:1, c0:c0 + w], True, True),
                        inc="s_mm")
            p.wait("vector", bmmk)
            cpm = p.op("vector", vcopy(arb(SLC, c0, w), ps[:, 0:w]), inc="s_v")
            ysbc.set_mark(bi, cpm)
            ysbc_m.append(cpm)
        ycor_mm = p.cur("s_mm")
        p.wait("gpsimd", ycor_mm)
        for cc in range(CH):
            dma("gpsimd", arb(D2B, cc * 1024, 1024),
                wA_inT[cc * 128:(cc + 1) * 128, 1024:2048], sem="s_w")
        for cc in range(CH):
            dma("gpsimd", arb(D2B, 8192 + cc * 1024, 1024),
                wA_outT[cc * 128:(cc + 1) * 128, :], sem="s_w")
        wv_mark = p.cur("s_w")
        p.wait("tensor", wv_mark)
        for mk in ysbc_m:
            p.wait("vector", mk)
        vtmp = [arb(MS0, 0, 512), arb(MS1, 0, 512)]
        vstate = [None, None]
        vpi = [0]

        def ev_v(jo, c0, w, ps, mmark):
            pi = vpi[0]
            vpi[0] = 1 - pi
            p.wait("scalar", mmark)
            p.wait("scalar", vstate[pi])
            am = p.op("scalar", act(vtmp[pi], ps, AF.Sigmoid,
                                    bias=t["binA"][:, 8 + jo:9 + jo]), inc="s_sc",
                      nd=vstate[pi] is not None)
            p.wait("vector", am)
            vm = p.op("vector", tt(ubuf(jo, c0, w), vtmp[pi],
                                   arb(SLC, c0, w), OP.mult), inc="s_v",
                      nd=vstate[pi] is not None)
            vstate[pi] = vm
            return vm
        emit_gemm(8, range(8),
                  lambda k, jo: arb(D2B, k * 1024 + jo * 128, 128),
                  lambda k, c0, w: xnc(k, c0, w), ev_v)
        y_done = p.cur("s_v")
        if DBG == "y":
            dump_bf(BB, 2056, 3)
            return None
        p.wait("tensor", y_done)

        evo_n = [0]

        def ev_o(jo, c0, w, ps, mmark):
            p.wait("vector", mmark)
            evo_n[0] += 1
            return p.op("vector", stt(res_c(jo, c0, w), ps,
                                      t["boA"][:, jo:jo + 1], res_c(jo, c0, w),
                                      OP.add, OP.add), inc="s_v", nd=evo_n[0] > 1)
        emit_gemm(8, range(8),
                  lambda k, jo: arb(D2B, 8192 + k * 1024 + jo * 128, 128),
                  lambda k, c0, w: ubuf(k, c0, w), ev_o)
        return p.cur("s_v"), p.cur("s_mm")

    def ffn(prev_v, prev_mm):
        W1R = [FWC, FWA, FWB, FWC]
        W2R = [FWB, FWC, FWA, FWB]

        def load_w1(k, waitmark):
            p.wait("gpsimd", waitmark)
            for cc in range(CH):
                dma("gpsimd", arb(W1R[k], cc * 1024, 1024),
                    f_w1T[cc * 128:(cc + 1) * 128, k * 1024:(k + 1) * 1024],
                    sem="s_w")
            return p.cur("s_w")

        def load_w2(k, waitmark):
            p.wait("gpsimd", waitmark)
            dma("gpsimd", ARB[:, W2R[k]:W2R[k] + 8192].rearrange(
                "p (c q) -> p c q", q=1024),
                f_w2T[k * 1024:(k + 1) * 1024, :].rearrange(
                    "(c p) q -> p c q", p=128), sem="s_w")
            return p.cur("s_w")

        w1m = [None] * 4
        w2m = [None] * 4
        p.wait("gpsimd", prev_v)
        p.wait("gpsimd", prev_mm)
        w1m[0] = load_w1(0, None)
        w2m[0] = load_w2(0, None)

        apply_m = ln_phase("plain")
        p.wait("gpsimd", p.cur("s_v"))
        oh_m = dma("gpsimd", t["oh16"], din_oh16, sem="s_w")
        p.wait("tensor", oh_m)
        for mk in apply_m:
            p.wait("tensor", mk)
        rt = Banks(p, [PF[0], PF[1]])
        lg_m = []
        for gi, (c0, w) in enumerate(NGS):
            bi, ps, bmk = rt.get()
            p.wait("tensor", bmk)
            mmark = None
            for k in range(CH):
                f = mm(ps[0:16, 0:w], t["rT"][:, k * 16:(k + 1) * 16],
                       xnc(k, c0, w), k == 0, k == CH - 1)
                if k == CH - 1:
                    mmark = p.op("tensor", f, inc="s_mm")
                else:
                    p.op("tensor", f)
            p.wait("vector", mmark)
            ev = p.op("vector", lambda e, o=TRANS[0:16, c0:c0 + w], i=ps[0:16, 0:w]:
                      e.tensor_scalar(o, i, t["rb"], None, OP.add), inc="s_v")
            rt.set_mark(bi, ev)
            lg_m.append(ev)
        for mk in lg_m:
            p.wait("tensor", mk)
        tms = None
        for g in range(16):
            tms = p.op("tensor", lambda e, g=g: e.transpose(
                PF[6][:, g * 16:(g + 1) * 16],
                TRANS[0:16, g * 128:(g + 1) * 128], t["eye16"]), inc="s_mm")
        p.wait("vector", tms)
        p.op("vector", vcopy(t["ltm"], PF[6][:, 0:256]))
        cur, other = t["ltm"], t["ltm2"]
        for r in range(4):
            p.op("vector", lambda e, i=cur: e.tensor_reduce(
                t["mx"], i.rearrange("p (g j) -> p g j", j=16), AX.X, OP.max))
            p.op("vector", lambda e, o=other[:, 0:128], m=t["mx"][:, 0:8],
                 i=cur[:, 0:128]: e.match_replace(o, m, i, NEG))
            p.op("vector", lambda e, o=other[:, 128:256], m=t["mx"][:, 8:16],
                 i=cur[:, 128:256]: e.match_replace(o, m, i, NEG))
            cur, other = other, cur
        mkm = p.op("vector", lambda e, i=cur: e.tensor_scalar(
            arb(FTM, 0, 256), i, NEG, None, OP.is_equal), inc="s_v")
        mask_done = None
        for wv in range(2):
            p.wait("tensor", mkm)
            if wv == 1:
                p.wait("tensor", mask_done)
            tbm = None
            for g in range(8):
                gg = wv * 8 + g
                tbm = p.op("tensor", lambda e, g=g, gg=gg: e.transpose(
                    PB7[0:16, g * 128:(g + 1) * 128],
                    arb(FTM, gg * 16, 16), t["eye128b"]), inc="s_mm")
            p.wait("vector", tbm)
            mask_done = p.op("vector", vcopy(
                ARB[0:16, FMF + wv * 1024:FMF + wv * 1024 + 1024],
                PB7[0:16, 0:1024]), inc="s_v")
        hb = Banks(p, [PF[0], PF[1], PF[6]])
        yb = Banks(p, [PF[2], PF[3]])
        mb = Banks(p, [PF[4], PF[5]])
        last_h_mm = [None] * 4
        last_y_mm = None
        prev_ti_y = None
        MBP = (FMB, FMB2)
        mask_guard = [None, None]

        def emit_mask_build(k, ti, par):
            marks = []
            p.wait("scalar", mask_guard[par])
            for b4 in range(4):
                b = 4 * k + b4
                bi, ps, bmk = mb.get()
                p.wait("tensor", bmk)
                mmk = p.op("tensor", mm(
                    ps[:, 0:512], t["oh16"][0:16, b * 128:(b + 1) * 128],
                    ARB[0:16, FMF + ti * 512:FMF + ti * 512 + 512], True, True),
                    inc="s_mm")
                p.wait("scalar", mmk)
                ev = p.op("scalar", act(arb(MBP[par], b4 * 512, 512),
                                        ps[:, 0:512], AF.Identity), inc="s_sc",
                          nd=(b4 > 0))
                mb.set_mark(bi, ev)
                marks.append(ev)
            return marks

        p.wait("tensor", mask_done)
        p.wait("tensor", oh_m)
        mpar = 0
        mbm = emit_mask_build(0, 0, 0)
        for k in range(4):
            if k + 1 < 4:
                w1m[k + 1] = load_w1(k + 1, last_y_mm)
            for ti in range(4):
                c0 = ti * 512
                cur_par = mpar
                p.wait("tensor", w1m[k])
                hm = None
                for sub in range(8):
                    bi, ps, bmk = hb.get()
                    p.wait("tensor", bmk)
                    mmark = None
                    for kc in range(CH):
                        f = mm(ps[:, 0:512], arb(W1R[k], kc * 1024 + sub * 128, 128),
                               xnc(kc, c0, 512), kc == 0, kc == CH - 1)
                        if kc == CH - 1:
                            mmark = p.op("tensor", f, inc="s_mm")
                        else:
                            p.op("tensor", f)
                    last_h_mm[k] = mmark
                    p.wait("scalar", mmark)
                    p.wait("scalar", prev_ti_y)
                    am = p.op("scalar", act(
                        arb(FH, sub * 512, 512), ps[:, 0:512], AF.Relu,
                        bias=t["fb1"][:, k * 8 + sub:k * 8 + sub + 1]), inc="s_sc",
                        nd=(sub > 0))
                    p.wait("vector", am)
                    for mk in mbm:
                        p.wait("vector", mk)
                    hm = p.op("vector", tt(arb(FH, sub * 512, 512),
                                           arb(FH, sub * 512, 512),
                                           arb(MBP[cur_par], (sub // 2) * 512, 512),
                                           OP.mult),
                              inc="s_v", nd=(sub > 0))
                    hb.set_mark(bi, am)
                mask_guard[cur_par] = hm
                p.wait("tensor", w2m[k])
                p.wait("tensor", hm)
                nk, nti = (k, ti + 1) if ti + 1 < 4 else (k + 1, 0)
                built_next = False
                for jo in range(8):
                    bi, ps, bmk = yb.get()
                    p.wait("tensor", bmk)
                    mmark = None
                    for sub in range(CH):
                        f = mm(ps[:, 0:512], arb(W2R[k], sub * 1024 + jo * 128, 128),
                               arb(FH, sub * 512, 512), sub == 0, sub == CH - 1)
                        if sub == CH - 1:
                            mmark = p.op("tensor", f, inc="s_mm")
                        else:
                            p.op("tensor", f)
                    last_y_mm = mmark
                    p.wait("vector", mmark)
                    if k == 0:
                        ev = p.op("vector", stt(res_c(jo, c0, 512), ps[:, 0:512],
                                                t["fb2"][:, jo:jo + 1],
                                                res_c(jo, c0, 512), OP.add, OP.add),
                                  inc="s_v", nd=(jo > 0))
                    else:
                        ev = p.op("vector", tt(res_c(jo, c0, 512), ps[:, 0:512],
                                               res_c(jo, c0, 512), OP.add),
                                  inc="s_v", nd=(jo > 0))
                    yb.set_mark(bi, ev)
                    if jo == 1 and nk < 4:
                        mbm = emit_mask_build(nk, nti, 1 - cur_par)
                        built_next = True
                if not built_next and nk < 4:
                    mbm = emit_mask_build(nk, nti, 1 - cur_par)
                mpar = 1 - cur_par
                prev_ti_y = last_y_mm
            if k + 1 < 4:
                w2m[k + 1] = load_w2(k + 1, last_h_mm[k])
        return p.cur("s_v"), p.cur("s_mm")

    def attn_b(prev_v, prev_mm):
        GATE = D2B
        apply_m = ln_phase("plain")
        ln_v = p.cur("s_v")
        for mk in apply_m:
            p.wait("tensor", mk)
        slc_bufs = [SLC, SLC + 1024]
        slice_mm = [None, None]
        prev_cm = [None, None]
        tails_mk = None
        for jo in range(16):
            pi = jo % 2
            p.wait("gpsimd", slice_mm[pi])
            if jo == 0:
                p.wait("gpsimd", prev_v)
                p.wait("gpsimd", prev_mm)
            if jo == 1:
                p.wait("gpsimd", ln_v)
            dma("gpsimd", arb(slc_bufs[pi], 0, 1024).rearrange(
                "p (c q) -> p c q", q=128),
                wB_inT[:, jo * 128:(jo + 1) * 128].rearrange(
                    "(c p) q -> p c q", p=128), sem="s_w")
            sld = p.cur("s_w")
            p.wait("tensor", sld)

            def ev_p(jjo, c0, w, ps, mmark, jo=jo):
                p.wait("scalar", mmark)
                if jo < 8:
                    return p.op("scalar", act(arb(GATE, jo * 2048 + c0, w), ps,
                                              AF.Gelu), inc="s_sc", nd=jo + c0 > 0)
                return p.op("scalar", act(ubuf(jo - 8, c0, w), ps, AF.Identity),
                            inc="s_sc", nd=True)
            emit_gemm(8, [jo],
                      lambda k, jjo, pi=pi: arb(slc_bufs[pi], k * 128, 128),
                      lambda k, c0, w: xnc(k, c0, w), ev_p)
            slice_mm[pi] = p.cur("s_mm")
            if jo >= 8:
                evm = p.cur("s_sc")
                tails_mk = emit_tails_chunk(jo - 8, evm)
                prev_cm[(jo - 8) % 2] = emit_conv_chunk(
                    jo - 8, t["wcB"], t["bcB"], evm, (jo - 8) % 2,
                    prev_cm[(jo - 8) % 2])
        rb3 = emit_ag(1, 3, tails_mk, True)
        halo_m = emit_conv_fixup(rb3, t["wcB"], t["bcB"])
        conv_m = [p.cur("s_sc")]
        gb = Banks(p, [PF[0], PF[1], PF[6]])
        gb.marks[2] = halo_m[-1]
        gslice = [SLC, SLC + 2048]
        slice_guard = [None, None]
        prev_cc_mm = None
        HB = (TR_A, TR_A2)
        LB = (TR_B, TR_B2)
        CB = (TR_C, TR_C2)
        IB = (MS2, MS0)
        traf_guard = [None, None]
        trb_guard = [None, None]
        sgi_guard = [None, None]
        for cc in range(CH):
            pi = cc % 2
            p.wait("gpsimd", slice_guard[pi])
            for gk in range(2):
                dma("gpsimd", arb(gslice[pi], gk * 1024, 1024).rearrange(
                    "p (c q) -> p c q", q=128),
                    wB_gT[:, gk * 1024 + cc * 128: gk * 1024 + (cc + 1) * 128]
                    .rearrange("(c p) q -> p c q", p=128), sem="s_w")
            gld = p.cur("s_w")
            p.wait("tensor", gld)
            if cc == 0:
                for cm in conv_m:
                    p.wait("tensor", cm)
            ev_f = [None, None]
            ev_i = [None, None]
            first_ev = cc == 0
            for half in range(2):
                hs = half * 1024
                TRA = TRANS[:, HB[half]:HB[half] + 1024]
                SGI = arb(IB[half], 0, 1024)
                for gk in range(2):
                    for sub in range(2):
                        c0 = hs + sub * 512
                        bi, ps, bmk = gb.get()
                        p.wait("tensor", bmk)
                        mmark = None
                        for k in range(CH):
                            f = mm(ps[:, 0:512],
                                   arb(gslice[pi], gk * 1024 + k * 128, 128),
                                   ubuf(k, c0, 512), k == 0, k == CH - 1)
                            if k == CH - 1:
                                mmark = p.op("tensor", f, inc="s_mm")
                            else:
                                p.op("tensor", f)
                        p.wait("scalar", mmark)
                        off = sub * 512
                        if gk == 0:
                            p.wait("scalar", traf_guard[half])
                            ev_f[half] = p.op("scalar", act(
                                TRA[:, off:off + 512], ps[:, 0:512],
                                AF.Sigmoid, bias=t["bgB"][:, cc:cc + 1]),
                                inc="s_sc", nd=not first_ev)
                            gb.set_mark(bi, ev_f[half])
                        else:
                            p.wait("scalar", sgi_guard[half])
                            ev_i[half] = p.op("scalar", act(
                                SGI[:, off:off + 512], ps[:, 0:512],
                                AF.Sigmoid, bias=t["bgB"][:, 8 + cc:9 + cc]),
                                inc="s_sc", nd=True)
                            gb.set_mark(bi, ev_i[half])
                        first_ev = False
            prev_cc_mm = p.cur("s_mm")
            slice_guard[pi] = prev_cc_mm
            TRA0, TRA1 = (TRANS[:, HB[h]:HB[h] + 1024] for h in (0, 1))
            TRB0, TRB1 = (TRANS[:, LB[h]:LB[h] + 1024] for h in (0, 1))
            TRC0, TRC1 = (TRANS[:, CB[h]:CB[h] + 1024] for h in (0, 1))
            SGI0, SGI1 = (arb(IB[h], 0, 1024) for h in (0, 1))
            TRH = TRANS[:, TR_H:TR_H + 1024]
            p.wait("vector", ev_f[0])
            p.wait("vector", trb_guard[0])
            lam0 = p.op("vector", lambda e, o=TRB0, i=TRA0, cc=cc:
                        e.tensor_scalar(o, i, t["sp8"][:, cc:cc + 1], None,
                                        OP.mult), inc="s_v", nd=True)
            p.wait("vector", ev_f[1])
            p.wait("vector", trb_guard[1])
            lam1 = p.op("vector", lambda e, o=TRB1, i=TRA1, cc=cc:
                        e.tensor_scalar(o, i, t["sp8"][:, cc:cc + 1], None,
                                        OP.mult), inc="s_v", nd=True)
            p.wait("scalar", lam0)
            p.op("scalar", act(TRA0, TRB0, AF.Exp))
            p.wait("scalar", lam1)
            p.op("scalar", act(TRA1, TRB1, AF.Exp), nd=True)
            sq0 = p.op("scalar", act(TRB0, TRA0, AF.Square), inc="s_sc")
            sq1 = p.op("scalar", act(TRB1, TRA1, AF.Square), inc="s_sc",
                       nd=True)
            p.wait("vector", sq0)
            b2m0 = p.op("vector", ts2(TRB0, TRB0, -1.0, 1.0 + 1e-6,
                                      OP.mult, OP.add), inc="s_v")
            p.wait("vector", sq1)
            b2m1 = p.op("vector", ts2(TRB1, TRB1, -1.0, 1.0 + 1e-6,
                                      OP.mult, OP.add), inc="s_v")
            p.wait("scalar", b2m0)
            btm0 = p.op("scalar", act(TRC0, TRB0, AF.Sqrt), inc="s_sc")
            p.wait("scalar", b2m1)
            btm1 = p.op("scalar", act(TRC1, TRB1, AF.Sqrt), inc="s_sc",
                        nd=True)
            trb_guard[0], trb_guard[1] = btm0, btm1
            p.wait("vector", btm0)
            p.wait("vector", ev_i[0])
            p.op("vector", tt(SGI0, TRC0, SGI0, OP.mult), nd=True)
            p.op("vector", tt(SGI0, SGI0, ubuf(cc, 0, 1024), OP.mult))
            sc1_0 = p.op("vector", scan(TRH, TRA0, SGI0, 0.0,
                                        OP.mult, OP.add), inc="s_v")
            sc2_0 = p.op("vector", scan(TRC0, TRA0, TRA0, 1.0,
                                        OP.mult, OP.bypass), inc="s_v", nd=True)
            p.op("vector", vcopy(t["hcar"][:, cc:cc + 1],
                                 TRH[:, 1023:1024]), nd=True)
            p.op("vector", vcopy(t["dcar"][:, cc:cc + 1],
                                 TRC0[:, 1023:1024]), nd=True)
            p.op("vector", tt(arb(XB, cc * 2048, 1024),
                              arb(GATE, cc * 2048, 1024), TRC0, OP.mult),
                 nd=True)
            p.op("vector", tt(arb(GATE, cc * 2048, 1024),
                              arb(GATE, cc * 2048, 1024), TRH, OP.mult),
                 nd=True)
            p.wait("vector", btm1)
            p.wait("vector", ev_i[1])
            p.op("vector", tt(SGI1, TRC1, SGI1, OP.mult), nd=True)
            p.op("vector", tt(SGI1, SGI1, ubuf(cc, 1024, 1024), OP.mult))
            sc1_1 = p.op("vector", scan(TRH, TRA1, SGI1,
                                        t["hcar"][:, cc:cc + 1],
                                        OP.mult, OP.add), inc="s_v")
            sc2_1 = p.op("vector", scan(TRC1, TRA1, TRA1,
                                        t["dcar"][:, cc:cc + 1],
                                        OP.mult, OP.bypass), inc="s_v", nd=True)
            p.op("vector", vcopy(t["hcar"][:, cc:cc + 1],
                                 TRH[:, 1023:1024]), nd=True)
            p.op("vector", vcopy(t["dcar"][:, cc:cc + 1],
                                 TRC1[:, 1023:1024]), nd=True)
            p.op("vector", tt(arb(XB, cc * 2048 + 1024, 1024),
                              arb(GATE, cc * 2048 + 1024, 1024), TRC1, OP.mult),
                 nd=True)
            p.op("vector", tt(arb(GATE, cc * 2048 + 1024, 1024),
                              arb(GATE, cc * 2048 + 1024, 1024), TRH, OP.mult),
                 nd=True)
            traf_guard[0], traf_guard[1] = sc2_0, sc2_1
            sgi_guard[0], sgi_guard[1] = sc1_0, sc1_1
        pyl_done = p.cur("s_v")
        p.op("vector", vcopy(t["sum24"][:, 0:8], t["dcar"]))
        sm = p.op("vector", vcopy(t["sum24"][:, 8:16], t["hcar"]), inc="s_v")
        rb4 = emit_ag(1, 2, sm, False)
        cmb_m = emit_combine(rb4)
        for mk in cmb_m:
            p.wait("vector", mk)
        cmb_v = p.cur("s_v")
        p.wait("gpsimd", prev_cc_mm)
        for cc in range(CH):
            dma("gpsimd", arb(BB, cc * 1024, 1024),
                wB_outT[cc * 128:(cc + 1) * 128, :], sem="s_w")
        wo_mark = p.cur("s_w")
        p.wait("tensor", wo_mark)
        p.wait("vector", pyl_done)
        ob = Banks(p, [PF[0], PF[1], PF[2], PF[3]])
        prev_ng_mm = None
        for gi, (c0, w) in enumerate(NGS):
            p.wait("vector", prev_ng_mm)
            ysl = None
            for cc in range(CH):
                ysl = p.op("vector", stt(arb(BB, 8192 + cc * 512, 512),
                                         arb(XB, cc * 2048 + c0, w),
                                         t["hp"][:, cc:cc + 1],
                                         arb(GATE, cc * 2048 + c0, w),
                                         OP.mult, OP.add), inc="s_v", nd=cc > 0)
            p.wait("tensor", ysl)
            for jo in range(8):
                bi, ps, bmk = ob.get()
                p.wait("tensor", bmk)
                mmark = None
                for k in range(CH):
                    f = mm(ps[:, 0:w], arb(BB, k * 1024 + jo * 128, 128),
                           arb(BB, 8192 + k * 512, 512), k == 0, k == CH - 1)
                    if k == CH - 1:
                        mmark = p.op("tensor", f, inc="s_mm")
                    else:
                        p.op("tensor", f)
                p.wait("vector", mmark)
                ev = p.op("vector", tt(res_c(jo, c0, w), ps, res_c(jo, c0, w),
                                       OP.add), inc="s_v", nd=jo > 0)
                ob.set_mark(bi, ev)
            prev_ng_mm = p.cur("s_mm")
        return p.cur("s_v"), p.cur("s_mm")

    done_v, done_mm = p.cur("s_v"), None
    dbg_exit = False
    if phase_limit >= 1:
        r = attn_a()
        if r is None:
            dbg_exit = True
        else:
            done_v, done_mm = r
    if not dbg_exit and phase_limit >= 2:
        done_v, done_mm = ffn(done_v, done_mm)
    if not dbg_exit and phase_limit >= 3:
        done_v, done_mm = attn_b(done_v, done_mm)
    if not dbg_exit and phase_limit >= 4:
        done_v, done_mm = ffn(done_v, done_mm)

    if not dbg_exit:
        p.wait("sync", done_v)
        p.wait("sync", done_mm)
        stm = dma("sync", out_d.rearrange("(c p) q -> p c q", p=128),
                  RES.rearrange("p (c q) -> p c q", q=T), sem="s_st")
        p.wait("sync", stm)

    import contextlib
    with contextlib.ExitStack() as stack:
        for s in ["s_x", "s_w", "s_mm", "s_sc", "s_v", "s_g", "s_cc", "s_ag", "s_st"]:
            p.sems[s] = stack.enter_context(nc.semaphore(s))
        with nc.Block() as block:
            @block.sync
            def _(e):
                for f in p.st["sync"]:
                    f(e)

            @block.tensor
            def _(e):
                for f in p.st["tensor"]:
                    f(e)

            @block.vector
            def _(e):
                for f in p.st["vector"]:
                    f(e)

            @block.scalar
            def _(e):
                for f in p.st["scalar"]:
                    f(e)

            @block.gpsimd
            def _(e):
                for f in p.st["gpsimd"]:
                    f(e)
    return nc


def _prep_inputs(inp):
    import ml_dtypes
    f32 = np.float32
    bf = ml_dtypes.bfloat16

    def ct(a):
        return np.ascontiguousarray(np.asarray(a, f32))

    def chunks8(v):
        return np.ascontiguousarray(np.asarray(v, f32).reshape(8, 128).T)

    def conv_lay(w):
        return np.ascontiguousarray(np.asarray(w, f32)[:, 0, :]
                                    .reshape(8, 128, 4).transpose(1, 0, 2)
                                    .reshape(128, 32))

    shared = {
        "wA_inT": ct(np.asarray(inp["L0_attn_a_w_in"]).T),
        "wA_dT": ct(np.asarray(inp["L0_attn_a_w_delta"]).T),
        "wA_BT": ct(np.asarray(inp["L0_attn_a_w_B"]).T),
        "wA_CT": ct(np.asarray(inp["L0_attn_a_w_C"]).T),
        "wA_outT": ct(np.asarray(inp["L0_attn_a_w_out"]).T),
        "wB_inT": ct(np.asarray(inp["L0_attn_b_w_in"]).T),
        "wB_gT": ct(np.asarray(inp["L0_attn_b_w_gates"]).T),
        "wB_outT": ct(np.asarray(inp["L0_attn_b_w_out"]).T),
        "f_w1T": ct(np.asarray(inp["L0_ffn_w1"]).T),
        "f_w2T": ct(np.asarray(inp["L0_ffn_w2"]).T),
        "binA": np.ascontiguousarray(ct(inp["L0_attn_a_b_in"]).reshape(16, 128).T),
        "lng": chunks8(inp["L0_attn_a_ln_g"]),
        "lnb": chunks8(inp["L0_attn_a_ln_b"]),
        "wcA": conv_lay(inp["L0_attn_a_w_conv"]),
        "bcA": chunks8(inp["L0_attn_a_b_conv"]),
        "bdA": chunks8(inp["L0_attn_a_b_delta"]),
        "bBA": chunks8(inp["L0_attn_a_b_B"]),
        "bCA": chunks8(inp["L0_attn_a_b_C"]),
        "boA": chunks8(inp["L0_attn_a_b_out"]),
        "wcB": conv_lay(inp["L0_attn_b_w_conv"]),
        "bcB": chunks8(inp["L0_attn_b_b_conv"]),
        "bgB": np.ascontiguousarray(ct(inp["L0_attn_b_b_gates"]).reshape(16, 128).T),
        "fb": chunks8(inp["L0_attn_b_forget_base"]),
        "rT": np.ascontiguousarray(ct(inp["L0_ffn_router_w"]).T
                                   .reshape(8, 128, 16).transpose(1, 0, 2)
                                   .reshape(128, 128)).astype(bf),
        "rb": ct(inp["L0_ffn_router_b"]).reshape(16, 1),
        "fb1": np.ascontiguousarray(ct(inp["L0_ffn_b1"]).reshape(32, 128).T),
        "fb2": chunks8(inp["L0_ffn_b2"]),
        "eye16": np.eye(16, dtype=f32),
        "eye128f": np.eye(128, dtype=f32),
        "eye128b": np.eye(128, dtype=bf),
    }
    oh = np.zeros((16, 2048), dtype=bf)
    for b in range(16):
        oh[b, b * 128:(b + 1) * 128] = 1
    shared["oh16"] = oh
    selAH = np.zeros((16, 16), f32)
    for i in range(8):
        selAH[2 * i, i] = 1.0
        selAH[2 * i + 1, 8 + i] = 1.0
    shared["selAH"] = selAH

    x = np.asarray(inp["L0_attn_a_x"], f32)[0]
    cores = []
    for c in range(NCORES):
        hs = np.zeros((24, 3), f32)
        if c > 0:
            for r in range(3):
                hs[(c - 1) * 3 + r, r] = 1.0
        sp = np.zeros((128, 8), f32)
        sp[:, :c] = 1.0
        cores.append({
            "x_fm": np.ascontiguousarray(x[c * T:(c + 1) * T].T),
            "hsel": hs,
            "spre": sp,
        })
    return shared, cores


def kernel(**inputs):
    shared, cores = _prep_inputs(inputs)
    nc = build_nc(PHASE_LIMIT)
    in_maps = []
    for c in range(NCORES):
        m = dict(shared)
        m.update(cores[c])
        in_maps.append(m)
    res = run_bass_kernel_spmd(nc, in_maps, core_ids=list(range(NCORES)))
    out = np.empty((1, NCORES * T, D), np.float32)
    for c in range(NCORES):
        out[0, c * T:(c + 1) * T] = res.results[c]["out_fm"].T
    return out


# version 13
# speedup vs baseline: 1.0032x; 1.0032x over previous
import numpy as np
import concourse.bass as bass
import concourse.mybir as mybir
from concourse.bass_utils import run_bass_kernel_spmd

F32 = mybir.dt.float32
BF16 = mybir.dt.bfloat16
AF = mybir.ActivationFunctionType
OP = mybir.AluOpType
AX = mybir.AxisListType

NCORES = 8
T = 2048
D = 1024
CH = 8
NGS = [(i * 512, 512) for i in range(4)]
EPS = 1e-5
NEG = -3.0e38

XB = 0
BB = 16384
D2B = 32832
FWA, FWB, FWC = 16384, 24576, 32768
FH = 41024
FMB = 45120
FMB2 = 51264
FMF = 47168
FTM = 54336
SLC = 49216
BC0 = 50240
BC1 = 52288
MS0 = 53312
MS1 = 54336
MS2 = 55360
X2A = 55360
X2B = 55872
ARB_COLS = 56384

TR_A, TR_B, TR_H, TR_C = 0, 1024, 2048, 3072
TR_ROWS = 4096
R_MU, R_T0, R_T1, R_R, R_MR = (TR_ROWS + i * 512 for i in range(5))
TR_A2, TR_B2, TR_C2 = 4096, 5120, 6656
TRANS_COLS = 7680

PHASE_LIMIT = 99
DBG = None


class Prog:
    ENGS = ("sync", "tensor", "vector", "scalar", "gpsimd")

    def __init__(self):
        self.st = {e: [] for e in self.ENGS}
        self.cnt = {}
        self.sems = {}

    def emit(self, eng, fn):
        self.st[eng].append(fn)

    def wait(self, eng, mark):
        if mark is None:
            return
        sem, val = mark
        if val <= 0:
            return
        self.emit(eng, lambda e, s=sem, v=val: e.wait_ge(self.sems[s], v))

    def cur(self, sem):
        return (sem, self.cnt.get(sem, 0))

    def op(self, eng, fn, inc=None, n=1, nd=False):
        if eng in ("vector", "scalar") and not nd:
            self.emit(eng, lambda e: e.drain())
        if inc is None:
            self.emit(eng, fn)
            return None
        self.cnt[inc] = self.cnt.get(inc, 0) + n
        tgt = self.cnt[inc]

        def g(e, fn=fn, s=inc, k=n):
            fn(e).then_inc(self.sems[s], k)
        self.emit(eng, g)
        return (inc, tgt)


class Banks:
    def __init__(self, prog, aps):
        self.p = prog
        self.aps = list(aps)
        self.marks = [None] * len(self.aps)
        self.i = 0

    def get(self):
        i = self.i % len(self.aps)
        self.i += 1
        return i, self.aps[i], self.marks[i]

    def set_mark(self, i, mark):
        self.marks[i] = mark


SMALLS = {
    "binA": ([128, 16], F32), "lng": ([128, 8], F32), "lnb": ([128, 8], F32),
    "wcA": ([128, 32], F32), "bcA": ([128, 8], F32), "bdA": ([128, 8], F32),
    "bBA": ([128, 8], F32), "bCA": ([128, 8], F32), "boA": ([128, 8], F32),
    "wcB": ([128, 32], F32), "bcB": ([128, 8], F32), "bgB": ([128, 16], F32),
    "fb": ([128, 8], F32), "rT": ([128, 128], BF16), "rb": ([16, 1], F32),
    "fb1": ([128, 32], F32), "fb2": ([128, 8], F32),
    "eye16": ([16, 16], F32), "eye128f": ([128, 128], F32),
    "eye128b": ([128, 128], BF16),
    "selAH": ([16, 16], F32), "hsel": ([24, 3], F32), "spre": ([128, 8], F32),
}
LATE = {"oh16": ([16, 2048], BF16)}

SCRATCH = {
    "nbdA": ([128, 8], F32), "sp8": ([128, 8], F32), "sptmp": ([128, 8], F32),
    "ones_row": ([1, 128], BF16), "ones_col": ([128, 1], BF16),
    "sum24": ([128, 24], F32), "tr24": ([24, 128], F32),
    "selev": ([128, 16], F32), "hcar": ([128, 8], F32), "dcar": ([128, 8], F32),
    "hp": ([128, 8], F32), "hpb": ([128, 8], BF16),
    "mx": ([128, 16], F32),
    "cmb": ([128, 24], F32),
    "sv24": ([128, 24], F32), "cv6": ([128, 8], F32), "cv3": ([128, 4], F32),
}


def build_nc(phase_limit):
    nc = bass.Bass("TRN2", target_bir_lowering=False)
    p = Prog()

    def din(name, shape, dt=F32):
        return nc.dram_tensor(name, shape, dt, kind="ExternalInput").ap()

    x_in = din("x_fm", [D, T])
    out_d = nc.dram_tensor("out_fm", [D, T], F32, kind="ExternalOutput").ap()

    wA_inT = din("wA_inT", [D, 2 * D])
    wA_dT = din("wA_dT", [D, D])
    wA_BT = din("wA_BT", [D, D])
    wA_CT = din("wA_CT", [D, D])
    wA_outT = din("wA_outT", [D, D])
    wB_inT = din("wB_inT", [D, 2 * D])
    wB_gT = din("wB_gT", [D, 2 * D])
    wB_outT = din("wB_outT", [D, D])
    f_w1T = din("f_w1T", [D, 4 * D])
    f_w2T = din("f_w2T", [4 * D, D])
    din_small = {k: din(k, s, d) for k, (s, d) in SMALLS.items()}
    din_oh16 = din("oh16", [16, 2048], BF16)

    agh_in = [nc.dram_tensor(f"agh{i}_in", [3, D], F32) for i in range(2)]
    agh_out = [nc.dram_tensor(f"agh{i}_out", [24, D], F32) for i in range(2)]
    ags_in = [nc.dram_tensor(f"ags{i}_in", [2, D], F32) for i in range(2)]
    ags_out = [nc.dram_tensor(f"ags{i}_out", [16, D], F32) for i in range(2)]

    RES = nc.alloc_sbuf_tensor("RES", [128, 16384], F32).ap()
    TRANS = nc.alloc_sbuf_tensor("TRANS", [128, TRANS_COLS], F32).ap()
    ARB = nc.alloc_sbuf_tensor("ARB", [128, ARB_COLS], BF16).ap()

    t = {k: nc.alloc_sbuf_tensor("t_" + k, s, d).ap() for k, (s, d) in SMALLS.items()}
    t["oh16"] = ARB[0:16, SLC:SLC + 2048]
    for k, (s, d) in SCRATCH.items():
        t[k] = nc.alloc_sbuf_tensor("t_" + k, s, d).ap()
    t["agh2"] = TRANS[0:24, 4096:5120]
    t["ags"] = TRANS[0:16, 0:1024]
    t["ysrow"] = TRANS[0:1, 4096:6144]
    t["yscor"] = TRANS[0:1, 1024:3072]
    t["ysbf"] = ARB[0:1, BC1:BC1 + 2048]
    t["rowseg"] = ARB[0:1, FMF:FMF + 1024]
    t["ltm"] = TRANS[:, 2048:2304]
    t["ltm2"] = TRANS[:, 2304:2560]

    PF = [nc.alloc_psum_tensor(f"pf{i}", [128, 512], F32).ap() for i in range(7)]
    PB7 = nc.alloc_psum_tensor("pb7", [128, 1024], BF16).ap()

    def res_c(cc, c0=0, w=T):
        return RES[:, cc * T + c0: cc * T + c0 + w]

    def arb(base, c0, w):
        return ARB[:, base + c0: base + c0 + w]

    def xnc(cc, c0=0, w=T):
        return arb(XB, cc * T + c0, w)

    def ubuf(cc, c0, w):
        return arb(BB, cc * 2056 + 3 + c0, w)

    def ubuf_raw(cc, c0, w):
        return arb(BB, cc * 2056 + c0, w)

    def dma(eng, out, in_, sem="s_x", noncontig=False, **kw):
        if noncontig:
            def fn(e, o=out, i=in_, kw=kw):
                with nc.allow_non_contiguous_dma("tiny"):
                    return e.dma_start(out=o, in_=i, **kw)
        else:
            def fn(e, o=out, i=in_, kw=kw):
                return e.dma_start(out=o, in_=i, **kw)
        return p.op(eng, fn, inc=sem, n=16)

    def mm(o, l, r, s, tt_):
        return lambda e, o=o, l=l, r=r, s=s, tt_=tt_: e.matmul(o, l, r, start=s, stop=tt_)

    def vcopy(out, in_):
        return lambda e, o=out, i=in_: e.tensor_copy(o, i)

    def act(out, in_, func, bias=None, scale=None):
        kw = {}
        if bias is not None:
            kw["bias"] = bias
        if scale is not None:
            kw["scale"] = scale
        return lambda e, o=out, i=in_, f=func, kw=kw: e.activation(out=o, in_=i, func=f, **kw)

    def tt(out, a, b, op):
        return lambda e, o=out, a=a, b=b, op=op: e.tensor_tensor(o, a, b, op)

    def ts2(out, a, s1, s2, op0, op1):
        return lambda e, o=out, a=a, s1=s1, s2=s2, op0=op0, op1=op1: \
            e.tensor_scalar(o, a, s1, s2, op0, op1)

    def ts1(out, a, s1, op0):
        return lambda e, o=out, a=a, s1=s1, op0=op0: e.tensor_scalar(o, a, s1, None, op0)

    def stt(out, in0, sc, in1, op0, op1):
        return lambda e, o=out, i0=in0, s=sc, i1=in1, op0=op0, op1=op1: \
            e.scalar_tensor_tensor(o, i0, s, i1, op0, op1)

    def scan(out, d0, d1, init, op0, op1):
        return lambda e, o=out, a=d0, b=d1, i=init, op0=op0, op1=op1: \
            e.tensor_tensor_scan(o, a, b, i, op0, op1)

    gemm_banks = Banks(p, [PF[0], PF[1], PF[2], PF[3]])

    def emit_gemm(K, JOs, lhsT_fn, rhs_fn, evac_fn, ngs=NGS, extraK=None, banks=None):
        banks = banks or gemm_banks
        for jo in JOs:
            for (c0, w) in ngs:
                bi, ps, bmark = banks.get()
                p.wait("tensor", bmark)
                mmark = None
                for k in range(K):
                    last = (k == K - 1) and extraK is None
                    f = mm(ps[:, 0:w], lhsT_fn(k, jo), rhs_fn(k, c0, w), k == 0, last)
                    if last:
                        mmark = p.op("tensor", f, inc="s_mm")
                    else:
                        p.op("tensor", f)
                if extraK is not None:
                    mmark = extraK(jo, c0, w, ps)
                emark = evac_fn(jo, c0, w, ps[:, 0:w], mmark)
                banks.set_mark(bi, emark)

    def ln_phase(kind):
        cast_m = []
        for cc in range(CH):
            cm = p.op("vector", vcopy(xnc(cc), res_c(cc)), inc="s_v", nd=cc > 0)
            cast_m.append(cm)
        bcb = Banks(p, [PF[0], PF[1], PF[2]])
        apply_deps = []
        x2segs = [ARB[:, X2A:X2A + 512], ARB[:, X2B:X2B + 512]]
        x2_free = [None, None]
        sumx_bank, sumsq_bank = PF[5], PF[6]
        sx_free = None
        sq_free = None
        for gi, (c0, w) in enumerate(NGS):
            p.wait("tensor", sx_free)
            sxm = None
            for cc in range(CH):
                if gi == 0:
                    p.wait("tensor", cast_m[cc])
                f = mm(sumx_bank[0:1, 0:w], t["ones_col"], xnc(cc, c0, w),
                       cc == 0, cc == CH - 1)
                if cc == CH - 1:
                    sxm = p.op("tensor", f, inc="s_mm")
                else:
                    p.op("tensor", f)
            p.wait("tensor", sq_free)
            sqgm = None
            for cc in range(CH):
                seg = x2segs[cc % 2]
                p.wait("scalar", cast_m[cc])
                p.wait("scalar", x2_free[cc % 2])
                sqm = p.op("scalar", act(seg, xnc(cc, c0, w), AF.Square),
                           inc="s_sc", nd=not (gi == 0 and cc == 0))
                p.wait("tensor", sqm)
                sqgm = p.op("tensor", mm(sumsq_bank[0:1, 0:w], t["ones_col"], seg,
                                         cc == 0, cc == CH - 1), inc="s_mm")
                x2_free[cc % 2] = sqgm
            def ROW(off, w=w):
                return TRANS[0:1, off:off + w]
            p.wait("vector", sxm)
            sx_free = p.op("vector", ts1(ROW(R_MU), sumx_bank[0:1, 0:w],
                                         1.0 / D, OP.mult), inc="s_v")
            p.op("vector", tt(ROW(R_T0), ROW(R_MU), ROW(R_MU), OP.mult))
            p.wait("vector", sqgm)
            p.op("vector", stt(ROW(R_T0), sumsq_bank[0:1, 0:w], 1.0 / D, ROW(R_T0),
                               OP.mult, OP.subtract))
            vm = p.op("vector", ts1(ROW(R_T0), ROW(R_T0), EPS, OP.add), inc="s_v")
            sq_free = vm
            p.wait("scalar", vm)
            p.op("scalar", act(ROW(R_T1), ROW(R_T0), AF.Ln))
            if kind == "plain":
                smk = p.op("scalar", act(ROW(R_R), ROW(R_T1), AF.Exp, scale=-0.5),
                           inc="s_sc")
                p.wait("vector", smk)
                p.op("vector", tt(ROW(R_MR), ROW(R_MU), ROW(R_R), OP.mult))
            else:
                r1m = p.op("scalar", act(ROW(R_R), ROW(R_T1), AF.Exp, scale=-0.5),
                           inc="s_sc")
                p.wait("vector", r1m)
                p.wait("vector", p.cur("s_mm"))
                sgm = p.op("vector", vcopy(t["rowseg"][0:1, 0:w], ROW(R_R)), inc="s_v")
                bi, ps, bmk = bcb.get()
                p.wait("tensor", bmk)
                p.wait("tensor", sgm)
                bmm = p.op("tensor", mm(ps[:, 0:w], t["ones_row"],
                                        t["rowseg"][0:1, 0:w], True, True), inc="s_mm")
                p.wait("vector", bmm)
                r1bc = p.op("vector", vcopy(arb(BB, c0, w), ps[:, 0:w]), inc="s_v")
                bcb.set_mark(bi, r1bc)
                apply_deps.append(r1bc)
                p.op("vector", tt(ROW(R_MR), ROW(R_MU), ROW(R_R), OP.mult))
                p.wait("vector", p.cur("s_mm"))
                sgm2 = p.op("vector", vcopy(t["rowseg"][0:1, 0:w], ROW(R_MR)),
                            inc="s_v")
                bi2, ps2, bmk2 = bcb.get()
                p.wait("tensor", bmk2)
                p.wait("tensor", sgm2)
                bmm2 = p.op("tensor", mm(ps2[:, 0:w], t["ones_row"],
                                         t["rowseg"][0:1, 0:w], True, True),
                            inc="s_mm")
                p.wait("vector", bmm2)
                m1bc = p.op("vector", vcopy(arb(BB, 2048 + c0, w), ps2[:, 0:w]),
                            inc="s_v")
                bcb.set_mark(bi2, m1bc)
                apply_deps.append(m1bc)
                sm2 = p.op("scalar", act(ROW(R_T0), ROW(R_T1), AF.Exp, scale=-1.0),
                           inc="s_sc")
                p.wait("vector", sm2)
                vm2 = p.op("vector", ts2(ROW(R_T0), ROW(R_T0), -EPS, 1.0 + EPS,
                                         OP.mult, OP.add), inc="s_v")
                p.wait("scalar", vm2)
                lm = p.op("scalar", act(ROW(R_T0), ROW(R_T0), AF.Ln), inc="s_sc")
                p.wait("vector", lm)
                vm3 = p.op("vector", tt(ROW(R_T0), ROW(R_T1), ROW(R_T0), OP.add),
                           inc="s_v")
                p.wait("scalar", vm3)
                smk = p.op("scalar", act(ROW(R_R), ROW(R_T0), AF.Exp, scale=-0.5),
                           inc="s_sc")
                p.wait("vector", smk)
                p.op("vector", tt(ROW(R_MR), ROW(R_MU), ROW(R_R), OP.mult))
            p.wait("vector", p.cur("s_mm"))
            c1 = p.op("vector", vcopy(t["rowseg"][0:1, 0:w], ROW(R_R)), inc="s_v")
            c2 = p.op("vector", vcopy(t["rowseg"][0:1, 512:512 + w], ROW(R_MR)),
                      inc="s_v")
            for (src_off, dst_base, dep) in ((0, BC0, c1), (512, BC1, c2)):
                bi, ps, bmk = bcb.get()
                p.wait("tensor", bmk)
                p.wait("tensor", dep)
                bmmk = p.op("tensor", mm(ps[:, 0:w], t["ones_row"],
                                         t["rowseg"][0:1, src_off:src_off + w],
                                         True, True), inc="s_mm")
                p.wait("vector", bmmk)
                cpm = p.op("vector", vcopy(arb(dst_base, c0, w), ps[:, 0:w]),
                           inc="s_v")
                bcb.set_mark(bi, cpm)
                apply_deps.append(cpm)
        stats_done = p.cur("s_mm")
        for mk in apply_deps:
            p.wait("vector", mk)
        p.wait("vector", stats_done)
        apply_m = []
        for cc in range(CH):
            if kind == "double":
                tmpb = arb(BB, 4096, T)
                p.op("vector", tt(tmpb, xnc(cc), arb(BB, 0, T), OP.mult),
                     nd=cc > 0)
                p.op("vector", tt(tmpb, tmpb, arb(BB, 2048, T), OP.subtract))
                p.op("vector", tt(res_c(cc), res_c(cc), tmpb, OP.add))
            p.op("vector", tt(xnc(cc), xnc(cc), arb(BC0, 0, T), OP.mult),
                 nd=(cc > 0 and kind != "double"))
            am = p.op("vector", tt(xnc(cc), xnc(cc), arb(BC1, 0, T), OP.subtract),
                      inc="s_v")
            if kind == "double":
                am = p.op("vector", lambda e, o=xnc(cc), cc=cc:
                          e.tensor_scalar(o, o, t["lng"][:, cc:cc + 1],
                                          t["lnb"][:, cc:cc + 1], OP.mult, OP.add),
                          inc="s_v")
            apply_m.append(am)
        return apply_m

    def emit_ag(idx, nrows, src_mark, halo):
        dram_in = (agh_in if halo else ags_in)[idx]
        dram_out = (agh_out if halo else ags_out)[idx]
        ncols = 8 * nrows
        p.wait("tensor", src_mark)
        p.wait("tensor", p.cur("s_sc"))
        tm = p.op("tensor", lambda e, n=ncols: e.transpose(
            PF[6][0:n, 0:128], t["sum24"][:, 0:n], t["eye128f"]), inc="s_mm")
        p.wait("vector", tm)
        vm = p.op("vector", vcopy(t["tr24"][0:ncols, :], PF[6][0:ncols, 0:128]),
                  inc="s_v")
        p.wait("gpsimd", vm)
        dm = dma("gpsimd", dram_in.ap().rearrange("r (c q) -> (r c) q", q=128),
                 t["tr24"][0:ncols, :], sem="s_g")
        p.wait("gpsimd", dm)
        ccm = p.op("gpsimd", lambda e, i=dram_in, o=dram_out: e.collective_compute(
            "AllGather", OP.bypass, replica_groups=[list(range(NCORES))],
            ins=[i.ap().opt()], outs=[o.ap().opt()]), inc="s_cc")
        dst = t["agh2"] if halo else t["ags"]
        p.wait("sync", ccm)
        rb = dma("sync", dst, dram_out.ap(), sem="s_ag")
        return rb

    def emit_tails_chunk(cc, evac_mark):
        p.wait("vector", evac_mark)
        mk = None
        for r in range(3):
            base = BB + cc * 2056
            p.op("vector", vcopy(t["sum24"][:, r * 8 + cc:r * 8 + cc + 1],
                                 ARB[:, base + 2048 + r:base + 2048 + r + 1]),
                 nd=True)
            mk = p.op("vector", vcopy(t["sv24"][:, r * 8 + cc:r * 8 + cc + 1],
                                      ARB[:, base + 3 + r:base + 3 + r + 1]),
                      inc="s_v", nd=True)
        return mk

    def emit_conv_chunk(cc, wc, bc, evac_mark, par, prev_cm):
        accw = TRANS[:, par * 2048:par * 2048 + 2048]
        p.wait("vector", evac_mark)
        p.wait("vector", prev_cm)
        p.op("vector", lambda e, o=accw, i=ubuf_raw(cc, 0, T),
             s=wc[:, cc * 4:cc * 4 + 1]: e.tensor_scalar(o, i, s, None, OP.mult))
        vm = None
        for k in range(1, 4):
            f = lambda e, o=accw, i=ubuf_raw(cc, k, T), \
                s=wc[:, cc * 4 + k:cc * 4 + k + 1]: \
                e.scalar_tensor_tensor(o, i, s, o, OP.mult, OP.add)
            if k == 3:
                vm = p.op("vector", f, inc="s_v")
            else:
                p.op("vector", f)
        p.wait("scalar", vm)
        cm = p.op("scalar", act(ubuf(cc, 0, T), accw, AF.Identity,
                                bias=bc[:, cc:cc + 1]), inc="s_sc", nd=True)
        return cm

    def emit_conv_fixup(rb_mark, wc, bc):
        p.wait("tensor", rb_mark)
        marks = []
        pm = None
        cm = None
        for cc in range(CH):
            p.wait("tensor", pm)
            tmm = p.op("tensor", mm(PF[6][:, 0:3],
                                    t["agh2"][0:24, cc * 128:(cc + 1) * 128],
                                    t["hsel"], True, True), inc="s_mm")
            p.wait("vector", tmm)
            p.wait("vector", cm)
            p.op("vector", vcopy(t["cv6"][:, 0:3], PF[6][:, 0:3]))
            for r in range(3):
                p.op("vector", vcopy(t["cv6"][:, 3 + r:4 + r],
                                     t["sv24"][:, r * 8 + cc:r * 8 + cc + 1]),
                     nd=True)
            p.op("vector", lambda e, cc=cc: e.tensor_scalar(
                t["cv3"][:, 0:3], t["cv6"][:, 0:3], wc[:, cc * 4:cc * 4 + 1],
                None, OP.mult))
            vm = None
            for k in range(1, 4):
                f = lambda e, k=k, cc=cc: e.scalar_tensor_tensor(
                    t["cv3"][:, 0:3], t["cv6"][:, k:k + 3],
                    wc[:, cc * 4 + k:cc * 4 + k + 1], t["cv3"][:, 0:3],
                    OP.mult, OP.add)
                if k == 3:
                    vm = p.op("vector", f, inc="s_v")
                else:
                    p.op("vector", f)
            pm = vm
            p.wait("scalar", vm)
            cm = p.op("scalar", act(ubuf(cc, 0, 3), t["cv3"][:, 0:3], AF.Identity,
                                    bias=bc[:, cc:cc + 1]), inc="s_sc")
            marks.append(cm)
        return marks

    def emit_combine(rb_mark):
        p.wait("tensor", rb_mark)
        marks = []
        pm = None
        for cc in range(CH):
            p.wait("tensor", pm)
            tmm = p.op("tensor", mm(PF[6][:, 0:16],
                                    t["ags"][0:16, cc * 128:(cc + 1) * 128],
                                    t["selAH"], True, True), inc="s_mm")
            p.wait("vector", tmm)
            pm = p.op("vector", vcopy(t["selev"], PF[6][:, 0:16]), inc="s_v")
            TA, TH, TS = t["cmb"][:, 0:8], t["cmb"][:, 8:16], t["cmb"][:, 16:24]
            p.op("vector", ts1(TA, t["selev"][:, 0:8], -1.0, OP.add))
            p.op("vector", tt(TA, TA, t["spre"], OP.mult))
            p.op("vector", ts1(TA, TA, 1.0, OP.add))
            p.op("vector", tt(TH, t["selev"][:, 8:16], t["spre"], OP.mult))
            p.op("vector", scan(TS, TA, TH, 0.0, OP.mult, OP.add))
            cm = p.op("vector", vcopy(t["hp"][:, cc:cc + 1], TS[:, 7:8]), inc="s_v")
            marks.append(cm)
        return marks

    dma("sync", RES.rearrange("p (c q) -> p c q", q=T),
        x_in.rearrange("(c p) q -> p c q", p=128), sem="s_x")
    for k in din_small:
        dma("sync", t[k], din_small[k], sem="s_x")
    ld0 = p.cur("s_x")
    p.wait("vector", ld0)
    p.op("vector", lambda e: e.memset(t["ones_row"], 1.0))
    p.op("vector", lambda e: e.memset(t["ones_col"], 1.0))
    for cc in range(CH):
        p.op("vector", lambda e, a=ARB[:, BB + cc * 2056:BB + cc * 2056 + 3]:
             e.memset(a, 0.0), nd=True)
    p.op("vector", ts1(t["nbdA"], t["bdA"], -1.0, OP.mult))
    rl = p.op("vector", ts1(t["sp8"], t["fb"], 0.0, OP.max), inc="s_v")
    p.wait("scalar", rl)
    p.op("scalar", act(t["sptmp"], t["fb"], AF.Abs))
    p.op("scalar", act(t["sptmp"], t["sptmp"], AF.Exp, scale=-1.0))
    lm0 = p.op("scalar", act(t["sptmp"], t["sptmp"], AF.Ln, bias=1.0), inc="s_sc")
    p.wait("vector", lm0)
    p.op("vector", tt(t["sp8"], t["sp8"], t["sptmp"], OP.add))
    p.op("vector", ts1(t["sp8"], t["sp8"], -8.0, OP.mult))
    WINU = D2B
    for cc in range(CH):
        dma("gpsimd", arb(WINU, cc * 1024, 1024),
            wA_inT[cc * 128:(cc + 1) * 128, 0:1024], sem="s_w")
    wu_mark = p.cur("s_w")

    def dump_bf(base, stride, off):
        p.wait("gpsimd", p.cur("s_v"))
        p.wait("gpsimd", p.cur("s_sc"))
        for cc in range(CH):
            dma("gpsimd", out_d[cc * 128:(cc + 1) * 128, :],
                ARB[:, base + cc * stride + off: base + cc * stride + off + T],
                sem="s_st")
        p.wait("sync", p.cur("s_st"))

    def dump_f32row(ap_row):
        p.wait("sync", p.cur("s_v"))
        dma("sync", out_d[0:1, :], ap_row, sem="s_st")
        p.wait("sync", p.cur("s_st"))

    def attn_a():
        apply_m = ln_phase("double")
        if DBG in ("bc0", "bc1"):
            dump_bf(BC0 if DBG == "bc0" else BC1, 0, 0)
            return None
        if DBG == "xn":
            for mk in apply_m:
                p.wait("gpsimd", mk)
            dump_bf(XB, T, 0)
            return None
        for mk in apply_m:
            p.wait("tensor", mk)
        p.wait("tensor", wu_mark)

        evu_n = [0]

        def ev_u(jo, c0, w, ps, mmark):
            p.wait("scalar", mmark)
            evu_n[0] += 1
            return p.op("scalar", act(ubuf(jo, c0, w), ps, AF.Identity,
                                      bias=t["binA"][:, jo:jo + 1]), inc="s_sc",
                        nd=evu_n[0] > 1)
        prev_cm = [None, None]
        tails_mk = None
        for jo in range(8):
            emit_gemm(8, [jo],
                      lambda k, jjo: arb(WINU, k * 1024 + jjo * 128, 128),
                      lambda k, c0, w: xnc(k, c0, w), ev_u)
            evm = p.cur("s_sc")
            if DBG == "u":
                continue
            tails_mk = emit_tails_chunk(jo, evm)
            prev_cm[jo % 2] = emit_conv_chunk(jo, t["wcA"], t["bcA"], evm,
                                              jo % 2, prev_cm[jo % 2])
        if DBG == "u":
            dump_bf(BB, 2056, 3)
            return None
        rb1 = emit_ag(0, 3, tails_mk, True)
        halo_m = emit_conv_fixup(rb1, t["wcA"], t["bcA"])
        conv_m = [p.cur("s_sc")]
        if DBG == "uc":
            for cm in conv_m:
                p.wait("gpsimd", cm)
            dump_bf(BB, 2056, 3)
            return None

        zbanks = Banks(p, [PF[0], PF[1], PF[6]])
        zbanks.marks[2] = halo_m[-1]
        ys_banks = [PF[2], PF[3], PF[4], PF[5]]
        ys_marks = [None] * 4
        prev_v_par = [None, None]
        A_P = (TR_A, TR_A2)
        B_P = (TR_B, TR_B2)
        C_P = (MS1, MS2)
        MS0_P = (MS0, SLC + 3072)
        ys_guard = [None, None]
        pend = [None]

        def flush_ys():
            if pend[0] is None:
                return
            pcc, phalf, ppar, phcm = pend[0]
            pend[0] = None
            for sub in range(2):
                gidx = phalf * 2 + sub
                p.wait("tensor", phcm)
                f = mm(ys_banks[gidx][0:1, 0:512], t["ones_col"],
                       arb(MS0_P[ppar], sub * 512, 512), pcc == 0, pcc == CH - 1)
                m = p.op("tensor", f, inc="s_mm")
                ys_guard[ppar] = m
                if pcc == CH - 1:
                    ys_marks[gidx] = m
        DBCR = [SLC + i * 1024 for i in range(3)]
        dbc_ring_guard = [None] * 3
        dbc_slice_m = [None] * 24
        ln_apply_v = p.cur("s_v")
        dbc_srcs = (wA_dT, wA_BT, wA_CT)

        def load_dbc(s):
            cc2, pi2 = s // 3, s % 3
            ri = s % 3
            p.wait("gpsimd", dbc_ring_guard[ri])
            if s < 3:
                p.wait("gpsimd", ln_apply_v)
            dma("gpsimd", arb(DBCR[ri], 0, 1024).rearrange(
                "p (c q) -> p c q", q=128),
                dbc_srcs[pi2][:, cc2 * 128:(cc2 + 1) * 128].rearrange(
                    "(c p) q -> p c q", p=128), sem=f"s_d{ri}")
            dbc_slice_m[s] = p.cur(f"s_d{ri}")
        for s in range(3):
            load_dbc(s)
        for cc in range(CH):
            if cc == 0:
                for cm in conv_m:
                    p.wait("tensor", cm)
            for half in range(2):
                hs = half * 1024
                par = (cc * 2 + half) % 2
                TRA = TRANS[:, A_P[par]:A_P[par] + 1024]
                TRB = TRANS[:, B_P[par]:B_P[par] + 1024]
                CSL = arb(C_P[par], 0, 1024)
                evC = None
                first_ev = True
                for pi in range(3):
                    s = cc * 3 + pi
                    if half == 0:
                        p.wait("tensor", dbc_slice_m[s])
                    for sub in range(2):
                        c0 = hs + sub * 512
                        bi, ps, bmk = zbanks.get()
                        p.wait("tensor", bmk)
                        mmark = None
                        for k in range(CH):
                            f = mm(ps[:, 0:512],
                                   arb(DBCR[s % 3], k * 128, 128),
                                   ubuf(k, c0, 512), k == 0, k == CH - 1)
                            if k == CH - 1:
                                mmark = p.op("tensor", f, inc="s_mm")
                            else:
                                p.op("tensor", f)
                        p.wait("scalar", mmark)
                        p.wait("scalar", prev_v_par[par])
                        off = sub * 512
                        if pi == 0:
                            emk = p.op("scalar", act(
                                TRA[:, off:off + 512], ps[:, 0:512],
                                AF.Sigmoid, bias=t["nbdA"][:, cc:cc + 1], scale=-1.0),
                                inc="s_sc", nd=not first_ev)
                        elif pi == 1:
                            emk = p.op("scalar", act(
                                TRB[:, off:off + 512], ps[:, 0:512],
                                AF.Identity, bias=t["bBA"][:, cc:cc + 1]),
                                inc="s_sc", nd=True)
                        else:
                            emk = p.op("scalar", act(
                                CSL[:, off:off + 512], ps[:, 0:512],
                                AF.Identity, bias=t["bCA"][:, cc:cc + 1]),
                                inc="s_sc", nd=True)
                            evC = emk
                        first_ev = False
                        zbanks.set_mark(bi, emk)
                    if half == 1:
                        dbc_ring_guard[s % 3] = p.cur("s_mm")
                        if s + 3 < 24:
                            load_dbc(s + 3)
                flush_ys()
                MS0B = arb(MS0_P[par], 0, 1024)
                p.wait("vector", evC)
                p.wait("vector", ys_guard[par])
                p.op("vector", tt(MS0B, ubuf(cc, hs, 1024),
                                  TRB, OP.mult))
                init_h = 0.0 if half == 0 else t["hcar"][:, cc:cc + 1]
                init_d = 1.0 if half == 0 else t["dcar"][:, cc:cc + 1]
                p.op("vector", scan(TRANS[:, TR_H:TR_H + 1024], TRA,
                                    MS0B, init_h, OP.mult, OP.add))
                p.op("vector", scan(TRANS[:, TR_C:TR_C + 1024], TRA, TRA, init_d,
                                    OP.mult, OP.bypass), nd=True)
                hcm = p.op("vector", tt(MS0B,
                                        TRANS[:, TR_H:TR_H + 1024],
                                        CSL, OP.mult), inc="s_v")
                pend[0] = (cc, half, par, hcm)
                p.op("vector", tt(arb(D2B, cc * 2048 + hs, 1024),
                                  TRANS[:, TR_C:TR_C + 1024],
                                  CSL, OP.mult), nd=True)
                p.op("vector", vcopy(t["hcar"][:, cc:cc + 1],
                                     TRANS[:, TR_H + 1023:TR_H + 1024]), nd=True)
                prev_v_par[par] = p.op("vector", vcopy(
                    t["dcar"][:, cc:cc + 1], TRANS[:, TR_C + 1023:TR_C + 1024]),
                    inc="s_v", nd=True)
        flush_ys()
        d2_all = p.cur("s_v")
        if DBG == "d2":
            dump_bf(D2B, 2048, 0)
            return None
        ysev = []
        for gidx in range(4):
            p.wait("vector", ys_marks[gidx])
            ev = p.op("vector", vcopy(t["ysrow"][0:1, gidx * 512:(gidx + 1) * 512],
                                      ys_banks[gidx][0:1, 0:512]), inc="s_v")
            ysev.append(ev)
        if DBG == "ysl":
            dump_f32row(t["ysrow"])
            return None
        p.op("vector", vcopy(t["sum24"][:, 0:8], t["dcar"]))
        sm = p.op("vector", vcopy(t["sum24"][:, 8:16], t["hcar"]), inc="s_v")
        rb2 = emit_ag(0, 2, sm, False)
        if DBG == "ags":
            p.wait("sync", rb2)
            p.wait("sync", p.cur("s_v"))
            stm_ = dma("sync", out_d[0:16, 0:1024], t["ags"], sem="s_st")
            stm_ = dma("sync", out_d[16:144, 1024:1048], t["sum24"], sem="s_st")
            p.wait("sync", stm_)
            return None
        cmb_m = emit_combine(rb2)
        for mk in cmb_m:
            p.wait("vector", mk)
        hpb_m = p.op("vector", vcopy(t["hpb"], t["hp"]), inc="s_v")
        if DBG == "hp":
            p.wait("sync", p.cur("s_v"))
            dma("sync", out_d[0:128, 0:8], t["hp"], sem="s_st")
            dma("sync", out_d[0:128, 8:16], t["spre"], sem="s_st")
            dma("sync", out_d[0:128, 16:40], t["cmb"], sem="s_st")
            stm_ = dma("sync", out_d[0:128, 40:56], t["selev"], sem="s_st")
            p.wait("sync", stm_)
            return None
        p.wait("tensor", hpb_m)
        p.wait("tensor", d2_all)
        yc = Banks(p, [PF[0], PF[1]])
        for gi, (c0, w) in enumerate(NGS):
            bi, ps, bmk = yc.get()
            p.wait("tensor", bmk)
            mmark = None
            for cc in range(CH):
                f = mm(ps[0:1, 0:w], t["hpb"][:, cc:cc + 1],
                       arb(D2B, cc * 2048 + c0, w), cc == 0, cc == CH - 1)
                if cc == CH - 1:
                    mmark = p.op("tensor", f, inc="s_mm")
                else:
                    p.op("tensor", f)
            p.wait("vector", mmark)
            ev = p.op("vector", vcopy(t["yscor"][0:1, c0:c0 + w], ps[0:1, 0:w]),
                      inc="s_v")
            yc.set_mark(bi, ev)
        for mk in ysev:
            p.wait("vector", mk)
        p.op("vector", tt(t["ysrow"], t["ysrow"], t["yscor"], OP.add))
        ysb = p.op("vector", vcopy(t["ysbf"], t["ysrow"]), inc="s_v")
        if DBG == "ys":
            dump_f32row(t["ysrow"])
            return None
        ysbc = Banks(p, [PF[0], PF[1]])
        ysbc_m = []
        for gi, (c0, w) in enumerate(NGS):
            bi, ps, bmk = ysbc.get()
            p.wait("tensor", bmk)
            p.wait("tensor", ysb)
            bmmk = p.op("tensor", mm(ps[:, 0:w], t["ones_row"],
                                     t["ysbf"][0:1, c0:c0 + w], True, True),
                        inc="s_mm")
            p.wait("vector", bmmk)
            cpm = p.op("vector", vcopy(arb(SLC, c0, w), ps[:, 0:w]), inc="s_v")
            ysbc.set_mark(bi, cpm)
            ysbc_m.append(cpm)
        ycor_mm = p.cur("s_mm")
        p.wait("gpsimd", ycor_mm)
        for cc in range(CH):
            dma("gpsimd", arb(D2B, cc * 1024, 1024),
                wA_inT[cc * 128:(cc + 1) * 128, 1024:2048], sem="s_w")
        for cc in range(CH):
            dma("gpsimd", arb(D2B, 8192 + cc * 1024, 1024),
                wA_outT[cc * 128:(cc + 1) * 128, :], sem="s_w")
        wv_mark = p.cur("s_w")
        p.wait("tensor", wv_mark)
        for mk in ysbc_m:
            p.wait("vector", mk)
            p.wait("tensor", mk)
        vtmp = [arb(MS0, 0, 512), arb(MS1, 0, 512)]
        vstate = [None, None]
        vpi = [0]

        def ev_v(jo, c0, w, ps, mmark):
            pi = vpi[0]
            vpi[0] = 1 - pi
            p.wait("scalar", mmark)
            p.wait("scalar", vstate[pi])
            am = p.op("scalar", act(vtmp[pi], ps, AF.Sigmoid,
                                    bias=t["binA"][:, 8 + jo:9 + jo]), inc="s_sc",
                      nd=vstate[pi] is not None)
            p.wait("vector", am)
            vm = p.op("vector", tt(ubuf(jo, c0, w), vtmp[pi],
                                   arb(SLC, c0, w), OP.mult), inc="s_v",
                      nd=vstate[pi] is not None)
            vstate[pi] = vm
            return vm
        emit_gemm(8, range(8),
                  lambda k, jo: arb(D2B, k * 1024 + jo * 128, 128),
                  lambda k, c0, w: xnc(k, c0, w), ev_v)
        y_done = p.cur("s_v")
        if DBG == "y":
            dump_bf(BB, 2056, 3)
            return None
        p.wait("tensor", y_done)

        evo_n = [0]

        def ev_o(jo, c0, w, ps, mmark):
            p.wait("vector", mmark)
            evo_n[0] += 1
            return p.op("vector", stt(res_c(jo, c0, w), ps,
                                      t["boA"][:, jo:jo + 1], res_c(jo, c0, w),
                                      OP.add, OP.add), inc="s_v", nd=evo_n[0] > 1)
        emit_gemm(8, range(8),
                  lambda k, jo: arb(D2B, 8192 + k * 1024 + jo * 128, 128),
                  lambda k, c0, w: ubuf(k, c0, w), ev_o)
        return p.cur("s_v"), p.cur("s_mm")

    def ffn(prev_v, prev_mm):
        W1R = [FWC, FWA, FWB, FWC]
        W2R = [FWB, FWC, FWA, FWB]

        def load_w1(k, waitmark):
            p.wait("gpsimd", waitmark)
            sname = "s_f1" if k % 2 == 0 else "s_f1b"
            for cc in range(CH):
                dma("gpsimd", arb(W1R[k], cc * 1024, 1024),
                    f_w1T[cc * 128:(cc + 1) * 128, k * 1024:(k + 1) * 1024],
                    sem=sname)
            return p.cur(sname)

        def load_w2(k, waitmark):
            p.wait("gpsimd", waitmark)
            dma("gpsimd", ARB[:, W2R[k]:W2R[k] + 8192].rearrange(
                "p (c q) -> p c q", q=1024),
                f_w2T[k * 1024:(k + 1) * 1024, :].rearrange(
                    "(c p) q -> p c q", p=128),
                sem="s_f2" if k % 2 == 0 else "s_f2b")
            return p.cur("s_f2" if k % 2 == 0 else "s_f2b")

        w1m = [None] * 4
        w2m = [None] * 4
        p.wait("gpsimd", prev_v)
        p.wait("gpsimd", prev_mm)
        w1m[0] = load_w1(0, None)
        w2m[0] = load_w2(0, None)

        apply_m = ln_phase("plain")
        p.wait("gpsimd", p.cur("s_v"))
        oh_m = dma("gpsimd", t["oh16"], din_oh16, sem="s_fo")
        p.wait("tensor", oh_m)
        for mk in apply_m:
            p.wait("tensor", mk)
        rt = Banks(p, [PF[0], PF[1]])
        lg_m = []
        for gi, (c0, w) in enumerate(NGS):
            bi, ps, bmk = rt.get()
            p.wait("tensor", bmk)
            mmark = None
            for k in range(CH):
                f = mm(ps[0:16, 0:w], t["rT"][:, k * 16:(k + 1) * 16],
                       xnc(k, c0, w), k == 0, k == CH - 1)
                if k == CH - 1:
                    mmark = p.op("tensor", f, inc="s_mm")
                else:
                    p.op("tensor", f)
            p.wait("vector", mmark)
            ev = p.op("vector", lambda e, o=TRANS[0:16, c0:c0 + w], i=ps[0:16, 0:w]:
                      e.tensor_scalar(o, i, t["rb"], None, OP.add), inc="s_v")
            rt.set_mark(bi, ev)
            lg_m.append(ev)
        for mk in lg_m:
            p.wait("tensor", mk)
        tms = None
        for g in range(16):
            tms = p.op("tensor", lambda e, g=g: e.transpose(
                PF[6][:, g * 16:(g + 1) * 16],
                TRANS[0:16, g * 128:(g + 1) * 128], t["eye16"]), inc="s_mm")
        p.wait("vector", tms)
        p.op("vector", vcopy(t["ltm"], PF[6][:, 0:256]))
        cur, other = t["ltm"], t["ltm2"]
        for r in range(4):
            p.op("vector", lambda e, i=cur: e.tensor_reduce(
                t["mx"], i.rearrange("p (g j) -> p g j", j=16), AX.X, OP.max))
            p.op("vector", lambda e, o=other[:, 0:128], m=t["mx"][:, 0:8],
                 i=cur[:, 0:128]: e.match_replace(o, m, i, NEG))
            p.op("vector", lambda e, o=other[:, 128:256], m=t["mx"][:, 8:16],
                 i=cur[:, 128:256]: e.match_replace(o, m, i, NEG))
            cur, other = other, cur
        mkm = p.op("vector", lambda e, i=cur: e.tensor_scalar(
            arb(FTM, 0, 256), i, NEG, None, OP.is_equal), inc="s_v")
        mask_done = None
        for wv in range(2):
            p.wait("tensor", mkm)
            if wv == 1:
                p.wait("tensor", mask_done)
            tbm = None
            for g in range(8):
                gg = wv * 8 + g
                tbm = p.op("tensor", lambda e, g=g, gg=gg: e.transpose(
                    PB7[0:16, g * 128:(g + 1) * 128],
                    arb(FTM, gg * 16, 16), t["eye128b"]), inc="s_mm")
            p.wait("vector", tbm)
            mask_done = p.op("vector", vcopy(
                ARB[0:16, FMF + wv * 1024:FMF + wv * 1024 + 1024],
                PB7[0:16, 0:1024]), inc="s_v")
        hb = Banks(p, [PF[0], PF[1], PF[6]])
        yb = Banks(p, [PF[2], PF[3]])
        mb = Banks(p, [PF[4], PF[5]])
        last_h_mm = [None] * 4
        last_y_mm = None
        prev_ti_y = None
        MBP = (FMB, FMB2)
        mask_guard = [None, None]

        def emit_mask_build(k, ti, par):
            marks = []
            p.wait("scalar", mask_guard[par])
            for b4 in range(4):
                b = 4 * k + b4
                bi, ps, bmk = mb.get()
                p.wait("tensor", bmk)
                mmk = p.op("tensor", mm(
                    ps[:, 0:512], t["oh16"][0:16, b * 128:(b + 1) * 128],
                    ARB[0:16, FMF + ti * 512:FMF + ti * 512 + 512], True, True),
                    inc="s_mm")
                p.wait("scalar", mmk)
                ev = p.op("scalar", act(arb(MBP[par], b4 * 512, 512),
                                        ps[:, 0:512], AF.Identity), inc="s_sc",
                          nd=(b4 > 0))
                mb.set_mark(bi, ev)
                marks.append(ev)
            return marks

        p.wait("tensor", mask_done)
        p.wait("tensor", oh_m)
        mpar = 0
        mbm = emit_mask_build(0, 0, 0)
        for k in range(4):
            if k + 1 < 4:
                w1m[k + 1] = load_w1(k + 1, last_y_mm)
            for ti in range(4):
                c0 = ti * 512
                cur_par = mpar
                p.wait("tensor", w1m[k])
                hm = None
                for sub in range(8):
                    bi, ps, bmk = hb.get()
                    p.wait("tensor", bmk)
                    mmark = None
                    for kc in range(CH):
                        f = mm(ps[:, 0:512], arb(W1R[k], kc * 1024 + sub * 128, 128),
                               xnc(kc, c0, 512), kc == 0, kc == CH - 1)
                        if kc == CH - 1:
                            mmark = p.op("tensor", f, inc="s_mm")
                        else:
                            p.op("tensor", f)
                    last_h_mm[k] = mmark
                    p.wait("scalar", mmark)
                    p.wait("scalar", prev_ti_y)
                    am = p.op("scalar", act(
                        arb(FH, sub * 512, 512), ps[:, 0:512], AF.Relu,
                        bias=t["fb1"][:, k * 8 + sub:k * 8 + sub + 1]), inc="s_sc",
                        nd=(sub > 0))
                    p.wait("vector", am)
                    for mk in mbm:
                        p.wait("vector", mk)
                    hm = p.op("vector", tt(arb(FH, sub * 512, 512),
                                           arb(FH, sub * 512, 512),
                                           arb(MBP[cur_par], (sub // 2) * 512, 512),
                                           OP.mult),
                              inc="s_v", nd=(sub > 0))
                    hb.set_mark(bi, am)
                mask_guard[cur_par] = hm
                p.wait("tensor", w2m[k])
                p.wait("tensor", hm)
                nk, nti = (k, ti + 1) if ti + 1 < 4 else (k + 1, 0)
                built_next = False
                for jo in range(8):
                    bi, ps, bmk = yb.get()
                    p.wait("tensor", bmk)
                    mmark = None
                    for sub in range(CH):
                        f = mm(ps[:, 0:512], arb(W2R[k], sub * 1024 + jo * 128, 128),
                               arb(FH, sub * 512, 512), sub == 0, sub == CH - 1)
                        if sub == CH - 1:
                            mmark = p.op("tensor", f, inc="s_mm")
                        else:
                            p.op("tensor", f)
                    last_y_mm = mmark
                    p.wait("vector", mmark)
                    if k == 0:
                        ev = p.op("vector", stt(res_c(jo, c0, 512), ps[:, 0:512],
                                                t["fb2"][:, jo:jo + 1],
                                                res_c(jo, c0, 512), OP.add, OP.add),
                                  inc="s_v", nd=(jo > 0))
                    else:
                        ev = p.op("vector", tt(res_c(jo, c0, 512), ps[:, 0:512],
                                               res_c(jo, c0, 512), OP.add),
                                  inc="s_v", nd=(jo > 0))
                    yb.set_mark(bi, ev)
                    if jo == 1 and nk < 4:
                        mbm = emit_mask_build(nk, nti, 1 - cur_par)
                        built_next = True
                if not built_next and nk < 4:
                    mbm = emit_mask_build(nk, nti, 1 - cur_par)
                mpar = 1 - cur_par
                prev_ti_y = last_y_mm
            if k + 1 < 4:
                w2m[k + 1] = load_w2(k + 1, last_h_mm[k])
        return p.cur("s_v"), p.cur("s_mm")

    def attn_b(prev_v, prev_mm):
        GATE = D2B
        apply_m = ln_phase("plain")
        ln_v = p.cur("s_v")
        for mk in apply_m:
            p.wait("tensor", mk)
        slc_bufs = [SLC, SLC + 1024]
        slice_mm = [None, None]
        prev_cm = [None, None]
        tails_mk = None
        for jo in range(16):
            pi = jo % 2
            p.wait("gpsimd", slice_mm[pi])
            if jo == 0:
                p.wait("gpsimd", prev_v)
                p.wait("gpsimd", prev_mm)
            if jo == 1:
                p.wait("gpsimd", ln_v)
            dma("gpsimd", arb(slc_bufs[pi], 0, 1024).rearrange(
                "p (c q) -> p c q", q=128),
                wB_inT[:, jo * 128:(jo + 1) * 128].rearrange(
                    "(c p) q -> p c q", p=128), sem=f"s_p{pi}")
            sld = p.cur(f"s_p{pi}")
            p.wait("tensor", sld)

            def ev_p(jjo, c0, w, ps, mmark, jo=jo):
                p.wait("scalar", mmark)
                if jo < 8:
                    return p.op("scalar", act(arb(GATE, jo * 2048 + c0, w), ps,
                                              AF.Gelu), inc="s_sc", nd=jo + c0 > 0)
                return p.op("scalar", act(ubuf(jo - 8, c0, w), ps, AF.Identity),
                            inc="s_sc", nd=True)
            emit_gemm(8, [jo],
                      lambda k, jjo, pi=pi: arb(slc_bufs[pi], k * 128, 128),
                      lambda k, c0, w: xnc(k, c0, w), ev_p)
            slice_mm[pi] = p.cur("s_mm")
            if jo >= 8:
                evm = p.cur("s_sc")
                tails_mk = emit_tails_chunk(jo - 8, evm)
                prev_cm[(jo - 8) % 2] = emit_conv_chunk(
                    jo - 8, t["wcB"], t["bcB"], evm, (jo - 8) % 2,
                    prev_cm[(jo - 8) % 2])
        rb3 = emit_ag(1, 3, tails_mk, True)
        halo_m = emit_conv_fixup(rb3, t["wcB"], t["bcB"])
        conv_m = [p.cur("s_sc")]
        gb = Banks(p, [PF[0], PF[1], PF[6]])
        gb.marks[2] = halo_m[-1]
        gslice = [SLC, SLC + 2048]
        slice_guard = [None, None]
        prev_cc_mm = None
        HB = (TR_A, TR_A2)
        LB = (TR_B, TR_B2)
        CB = (TR_C, TR_C2)
        IB = (MS2, MS0)
        traf_guard = [None, None]
        trb_guard = [None, None]
        sgi_guard = [None, None]
        for cc in range(CH):
            pi = cc % 2
            p.wait("gpsimd", slice_guard[pi])
            for gk in range(2):
                dma("gpsimd", arb(gslice[pi], gk * 1024, 1024).rearrange(
                    "p (c q) -> p c q", q=128),
                    wB_gT[:, gk * 1024 + cc * 128: gk * 1024 + (cc + 1) * 128]
                    .rearrange("(c p) q -> p c q", p=128), sem=f"s_b{pi}")
            gld = p.cur(f"s_b{pi}")
            p.wait("tensor", gld)
            if cc == 0:
                for cm in conv_m:
                    p.wait("tensor", cm)
            ev_f = [None, None]
            ev_i = [None, None]
            first_ev = cc == 0
            for half in range(2):
                hs = half * 1024
                TRA = TRANS[:, HB[half]:HB[half] + 1024]
                SGI = arb(IB[half], 0, 1024)
                for gk in range(2):
                    for sub in range(2):
                        c0 = hs + sub * 512
                        bi, ps, bmk = gb.get()
                        p.wait("tensor", bmk)
                        mmark = None
                        for k in range(CH):
                            f = mm(ps[:, 0:512],
                                   arb(gslice[pi], gk * 1024 + k * 128, 128),
                                   ubuf(k, c0, 512), k == 0, k == CH - 1)
                            if k == CH - 1:
                                mmark = p.op("tensor", f, inc="s_mm")
                            else:
                                p.op("tensor", f)
                        p.wait("scalar", mmark)
                        off = sub * 512
                        if gk == 0:
                            p.wait("scalar", traf_guard[half])
                            ev_f[half] = p.op("scalar", act(
                                TRA[:, off:off + 512], ps[:, 0:512],
                                AF.Sigmoid, bias=t["bgB"][:, cc:cc + 1]),
                                inc="s_sc", nd=not first_ev)
                            gb.set_mark(bi, ev_f[half])
                        else:
                            p.wait("scalar", sgi_guard[half])
                            ev_i[half] = p.op("scalar", act(
                                SGI[:, off:off + 512], ps[:, 0:512],
                                AF.Sigmoid, bias=t["bgB"][:, 8 + cc:9 + cc]),
                                inc="s_sc", nd=True)
                            gb.set_mark(bi, ev_i[half])
                        first_ev = False
            prev_cc_mm = p.cur("s_mm")
            slice_guard[pi] = prev_cc_mm
            TRA0, TRA1 = (TRANS[:, HB[h]:HB[h] + 1024] for h in (0, 1))
            TRB0, TRB1 = (TRANS[:, LB[h]:LB[h] + 1024] for h in (0, 1))
            TRC0, TRC1 = (TRANS[:, CB[h]:CB[h] + 1024] for h in (0, 1))
            SGI0, SGI1 = (arb(IB[h], 0, 1024) for h in (0, 1))
            TRH = TRANS[:, TR_H:TR_H + 1024]
            p.wait("vector", ev_f[0])
            p.wait("vector", trb_guard[0])
            lam0 = p.op("vector", lambda e, o=TRB0, i=TRA0, cc=cc:
                        e.tensor_scalar(o, i, t["sp8"][:, cc:cc + 1], None,
                                        OP.mult), inc="s_v", nd=True)
            p.wait("vector", ev_f[1])
            p.wait("vector", trb_guard[1])
            lam1 = p.op("vector", lambda e, o=TRB1, i=TRA1, cc=cc:
                        e.tensor_scalar(o, i, t["sp8"][:, cc:cc + 1], None,
                                        OP.mult), inc="s_v", nd=True)
            p.wait("scalar", lam0)
            p.op("scalar", act(TRA0, TRB0, AF.Exp))
            p.wait("scalar", lam1)
            p.op("scalar", act(TRA1, TRB1, AF.Exp), nd=True)
            sq0 = p.op("scalar", act(TRB0, TRA0, AF.Square), inc="s_sc")
            sq1 = p.op("scalar", act(TRB1, TRA1, AF.Square), inc="s_sc",
                       nd=True)
            p.wait("vector", sq0)
            b2m0 = p.op("vector", ts2(TRB0, TRB0, -1.0, 1.0 + 1e-6,
                                      OP.mult, OP.add), inc="s_v")
            p.wait("vector", sq1)
            b2m1 = p.op("vector", ts2(TRB1, TRB1, -1.0, 1.0 + 1e-6,
                                      OP.mult, OP.add), inc="s_v")
            p.wait("scalar", b2m0)
            btm0 = p.op("scalar", act(TRC0, TRB0, AF.Sqrt), inc="s_sc")
            p.wait("scalar", b2m1)
            btm1 = p.op("scalar", act(TRC1, TRB1, AF.Sqrt), inc="s_sc",
                        nd=True)
            trb_guard[0], trb_guard[1] = btm0, btm1
            p.wait("vector", btm0)
            p.wait("vector", ev_i[0])
            p.op("vector", tt(SGI0, TRC0, SGI0, OP.mult), nd=True)
            p.op("vector", tt(SGI0, SGI0, ubuf(cc, 0, 1024), OP.mult))
            sc1_0 = p.op("vector", scan(TRH, TRA0, SGI0, 0.0,
                                        OP.mult, OP.add), inc="s_v")
            sc2_0 = p.op("vector", scan(TRC0, TRA0, TRA0, 1.0,
                                        OP.mult, OP.bypass), inc="s_v", nd=True)
            p.op("vector", vcopy(t["hcar"][:, cc:cc + 1],
                                 TRH[:, 1023:1024]))
            p.op("vector", vcopy(t["dcar"][:, cc:cc + 1],
                                 TRC0[:, 1023:1024]), nd=True)
            p.op("vector", tt(arb(XB, cc * 2048, 1024),
                              arb(GATE, cc * 2048, 1024), TRC0, OP.mult),
                 nd=True)
            p.op("vector", tt(arb(GATE, cc * 2048, 1024),
                              arb(GATE, cc * 2048, 1024), TRH, OP.mult),
                 nd=True)
            p.wait("vector", btm1)
            p.wait("vector", ev_i[1])
            p.op("vector", tt(SGI1, TRC1, SGI1, OP.mult), nd=True)
            p.op("vector", tt(SGI1, SGI1, ubuf(cc, 1024, 1024), OP.mult))
            sc1_1 = p.op("vector", scan(TRH, TRA1, SGI1,
                                        t["hcar"][:, cc:cc + 1],
                                        OP.mult, OP.add), inc="s_v")
            sc2_1 = p.op("vector", scan(TRC1, TRA1, TRA1,
                                        t["dcar"][:, cc:cc + 1],
                                        OP.mult, OP.bypass), inc="s_v", nd=True)
            p.op("vector", vcopy(t["hcar"][:, cc:cc + 1],
                                 TRH[:, 1023:1024]))
            p.op("vector", vcopy(t["dcar"][:, cc:cc + 1],
                                 TRC1[:, 1023:1024]), nd=True)
            p.op("vector", tt(arb(XB, cc * 2048 + 1024, 1024),
                              arb(GATE, cc * 2048 + 1024, 1024), TRC1, OP.mult),
                 nd=True)
            p.op("vector", tt(arb(GATE, cc * 2048 + 1024, 1024),
                              arb(GATE, cc * 2048 + 1024, 1024), TRH, OP.mult),
                 nd=True)
            traf_guard[0], traf_guard[1] = sc2_0, sc2_1
            sgi_guard[0], sgi_guard[1] = sc1_0, sc1_1
        pyl_done = p.cur("s_v")
        p.op("vector", vcopy(t["sum24"][:, 0:8], t["dcar"]))
        sm = p.op("vector", vcopy(t["sum24"][:, 8:16], t["hcar"]), inc="s_v")
        rb4 = emit_ag(1, 2, sm, False)
        cmb_m = emit_combine(rb4)
        for mk in cmb_m:
            p.wait("vector", mk)
        cmb_v = p.cur("s_v")
        p.wait("gpsimd", prev_cc_mm)
        for cc in range(CH):
            dma("gpsimd", arb(BB, cc * 1024, 1024),
                wB_outT[cc * 128:(cc + 1) * 128, :], sem="s_w")
        wo_mark = p.cur("s_w")
        p.wait("tensor", wo_mark)
        p.wait("vector", pyl_done)
        ob = Banks(p, [PF[0], PF[1], PF[2], PF[3]])
        prev_ng_mm = None
        for gi, (c0, w) in enumerate(NGS):
            p.wait("vector", prev_ng_mm)
            ysl = None
            for cc in range(CH):
                ysl = p.op("vector", stt(arb(BB, 8192 + cc * 512, 512),
                                         arb(XB, cc * 2048 + c0, w),
                                         t["hp"][:, cc:cc + 1],
                                         arb(GATE, cc * 2048 + c0, w),
                                         OP.mult, OP.add), inc="s_v", nd=cc > 0)
            p.wait("tensor", ysl)
            for jo in range(8):
                bi, ps, bmk = ob.get()
                p.wait("tensor", bmk)
                mmark = None
                for k in range(CH):
                    f = mm(ps[:, 0:w], arb(BB, k * 1024 + jo * 128, 128),
                           arb(BB, 8192 + k * 512, 512), k == 0, k == CH - 1)
                    if k == CH - 1:
                        mmark = p.op("tensor", f, inc="s_mm")
                    else:
                        p.op("tensor", f)
                p.wait("vector", mmark)
                ev = p.op("vector", tt(res_c(jo, c0, w), ps, res_c(jo, c0, w),
                                       OP.add), inc="s_v", nd=jo > 0)
                ob.set_mark(bi, ev)
            prev_ng_mm = p.cur("s_mm")
        return p.cur("s_v"), p.cur("s_mm")

    done_v, done_mm = p.cur("s_v"), None
    dbg_exit = False
    if phase_limit >= 1:
        r = attn_a()
        if r is None:
            dbg_exit = True
        else:
            done_v, done_mm = r
    if not dbg_exit and phase_limit >= 2:
        done_v, done_mm = ffn(done_v, done_mm)
    if not dbg_exit and phase_limit >= 3:
        done_v, done_mm = attn_b(done_v, done_mm)
    if not dbg_exit and phase_limit >= 4:
        done_v, done_mm = ffn(done_v, done_mm)

    if not dbg_exit:
        p.wait("sync", done_v)
        p.wait("sync", done_mm)
        stm = dma("sync", out_d.rearrange("(c p) q -> p c q", p=128),
                  RES.rearrange("p (c q) -> p c q", q=T), sem="s_st")
        p.wait("sync", stm)

    import contextlib
    with contextlib.ExitStack() as stack:
        for s in ["s_x", "s_w", "s_mm", "s_sc", "s_v", "s_g", "s_cc", "s_ag",
                  "s_st", "s_d0", "s_d1", "s_d2", "s_f1", "s_f2", "s_fo",
                  "s_f1b", "s_f2b", "s_p0", "s_p1", "s_b0", "s_b1"]:
            p.sems[s] = stack.enter_context(nc.semaphore(s))
        with nc.Block() as block:
            @block.sync
            def _(e):
                for f in p.st["sync"]:
                    f(e)

            @block.tensor
            def _(e):
                for f in p.st["tensor"]:
                    f(e)

            @block.vector
            def _(e):
                for f in p.st["vector"]:
                    f(e)

            @block.scalar
            def _(e):
                for f in p.st["scalar"]:
                    f(e)

            @block.gpsimd
            def _(e):
                for f in p.st["gpsimd"]:
                    f(e)
    return nc


def _prep_inputs(inp):
    import ml_dtypes
    f32 = np.float32
    bf = ml_dtypes.bfloat16

    def ct(a):
        return np.ascontiguousarray(np.asarray(a, f32))

    def chunks8(v):
        return np.ascontiguousarray(np.asarray(v, f32).reshape(8, 128).T)

    def conv_lay(w):
        return np.ascontiguousarray(np.asarray(w, f32)[:, 0, :]
                                    .reshape(8, 128, 4).transpose(1, 0, 2)
                                    .reshape(128, 32))

    shared = {
        "wA_inT": ct(np.asarray(inp["L0_attn_a_w_in"]).T),
        "wA_dT": ct(np.asarray(inp["L0_attn_a_w_delta"]).T),
        "wA_BT": ct(np.asarray(inp["L0_attn_a_w_B"]).T),
        "wA_CT": ct(np.asarray(inp["L0_attn_a_w_C"]).T),
        "wA_outT": ct(np.asarray(inp["L0_attn_a_w_out"]).T),
        "wB_inT": ct(np.asarray(inp["L0_attn_b_w_in"]).T),
        "wB_gT": ct(np.asarray(inp["L0_attn_b_w_gates"]).T),
        "wB_outT": ct(np.asarray(inp["L0_attn_b_w_out"]).T),
        "f_w1T": ct(np.asarray(inp["L0_ffn_w1"]).T),
        "f_w2T": ct(np.asarray(inp["L0_ffn_w2"]).T),
        "binA": np.ascontiguousarray(ct(inp["L0_attn_a_b_in"]).reshape(16, 128).T),
        "lng": chunks8(inp["L0_attn_a_ln_g"]),
        "lnb": chunks8(inp["L0_attn_a_ln_b"]),
        "wcA": conv_lay(inp["L0_attn_a_w_conv"]),
        "bcA": chunks8(inp["L0_attn_a_b_conv"]),
        "bdA": chunks8(inp["L0_attn_a_b_delta"]),
        "bBA": chunks8(inp["L0_attn_a_b_B"]),
        "bCA": chunks8(inp["L0_attn_a_b_C"]),
        "boA": chunks8(inp["L0_attn_a_b_out"]),
        "wcB": conv_lay(inp["L0_attn_b_w_conv"]),
        "bcB": chunks8(inp["L0_attn_b_b_conv"]),
        "bgB": np.ascontiguousarray(ct(inp["L0_attn_b_b_gates"]).reshape(16, 128).T),
        "fb": chunks8(inp["L0_attn_b_forget_base"]),
        "rT": np.ascontiguousarray(ct(inp["L0_ffn_router_w"]).T
                                   .reshape(8, 128, 16).transpose(1, 0, 2)
                                   .reshape(128, 128)).astype(bf),
        "rb": ct(inp["L0_ffn_router_b"]).reshape(16, 1),
        "fb1": np.ascontiguousarray(ct(inp["L0_ffn_b1"]).reshape(32, 128).T),
        "fb2": chunks8(inp["L0_ffn_b2"]),
        "eye16": np.eye(16, dtype=f32),
        "eye128f": np.eye(128, dtype=f32),
        "eye128b": np.eye(128, dtype=bf),
    }
    oh = np.zeros((16, 2048), dtype=bf)
    for b in range(16):
        oh[b, b * 128:(b + 1) * 128] = 1
    shared["oh16"] = oh
    selAH = np.zeros((16, 16), f32)
    for i in range(8):
        selAH[2 * i, i] = 1.0
        selAH[2 * i + 1, 8 + i] = 1.0
    shared["selAH"] = selAH

    x = np.asarray(inp["L0_attn_a_x"], f32)[0]
    cores = []
    for c in range(NCORES):
        hs = np.zeros((24, 3), f32)
        if c > 0:
            for r in range(3):
                hs[(c - 1) * 3 + r, r] = 1.0
        sp = np.zeros((128, 8), f32)
        sp[:, :c] = 1.0
        cores.append({
            "x_fm": np.ascontiguousarray(x[c * T:(c + 1) * T].T),
            "hsel": hs,
            "spre": sp,
        })
    return shared, cores


def kernel(**inputs):
    shared, cores = _prep_inputs(inputs)
    nc = build_nc(PHASE_LIMIT)
    in_maps = []
    for c in range(NCORES):
        m = dict(shared)
        m.update(cores[c])
        in_maps.append(m)
    res = run_bass_kernel_spmd(nc, in_maps, core_ids=list(range(NCORES)))
    out = np.empty((1, NCORES * T, D), np.float32)
    for c in range(NCORES):
        out[0, c * T:(c + 1) * T] = res.results[c]["out_fm"].T
    return out


# version 15
# speedup vs baseline: 1.0388x; 1.0355x over previous
import numpy as np
import concourse.bass as bass
import concourse.mybir as mybir
from concourse.bass_utils import run_bass_kernel_spmd

F32 = mybir.dt.float32
BF16 = mybir.dt.bfloat16
AF = mybir.ActivationFunctionType
OP = mybir.AluOpType
AX = mybir.AxisListType

NCORES = 8
T = 2048
D = 1024
CH = 8
NGS = [(i * 512, 512) for i in range(4)]
EPS = 1e-5
NEG = -3.0e38

XB = 0
BB = 16384
D2B = 32832
FWA, FWB, FWC = 16384, 24576, 32768
FH = 41024
FMB = 45120
FMB2 = 51264
FMF = 47168
FTM = 54336
SLC = 49216
BC0 = 50240
BC1 = 52288
MS0 = 53312
MS1 = 54336
MS2 = 55360
X2A = 55360
X2B = 55872
ARB_COLS = 56384

TR_A, TR_B, TR_H, TR_C = 0, 1024, 2048, 3072
TR_ROWS = 4096
R_MU, R_T0, R_T1, R_R, R_MR = (TR_ROWS + i * 512 for i in range(5))
TR_A2, TR_B2, TR_C2 = 4096, 5120, 6656
TRANS_COLS = 7680

PHASE_LIMIT = 99
DBG = None


class Prog:
    ENGS = ("sync", "tensor", "vector", "scalar", "gpsimd")

    def __init__(self):
        self.st = {e: [] for e in self.ENGS}
        self.cnt = {}
        self.sems = {}

    def emit(self, eng, fn):
        self.st[eng].append(fn)

    def wait(self, eng, mark):
        if mark is None:
            return
        sem, val = mark
        if val <= 0:
            return
        self.emit(eng, lambda e, s=sem, v=val: e.wait_ge(self.sems[s], v))

    def cur(self, sem):
        return (sem, self.cnt.get(sem, 0))

    def op(self, eng, fn, inc=None, n=1, nd=False):
        if eng in ("vector", "scalar") and not nd:
            self.emit(eng, lambda e: e.drain())
        if inc is None:
            self.emit(eng, fn)
            return None
        self.cnt[inc] = self.cnt.get(inc, 0) + n
        tgt = self.cnt[inc]

        def g(e, fn=fn, s=inc, k=n):
            fn(e).then_inc(self.sems[s], k)
        self.emit(eng, g)
        return (inc, tgt)


class Banks:
    def __init__(self, prog, aps):
        self.p = prog
        self.aps = list(aps)
        self.marks = [None] * len(self.aps)
        self.i = 0

    def get(self):
        i = self.i % len(self.aps)
        self.i += 1
        return i, self.aps[i], self.marks[i]

    def set_mark(self, i, mark):
        self.marks[i] = mark


SMALLS = {
    "binA": ([128, 16], F32), "lng": ([128, 8], F32), "lnb": ([128, 8], F32),
    "wcA": ([128, 32], F32), "bcA": ([128, 8], F32), "bdA": ([128, 8], F32),
    "bBA": ([128, 8], F32), "bCA": ([128, 8], F32), "boA": ([128, 8], F32),
    "wcB": ([128, 32], F32), "bcB": ([128, 8], F32), "bgB": ([128, 16], F32),
    "fb": ([128, 8], F32), "rT": ([128, 128], BF16), "rb": ([16, 1], F32),
    "fb1": ([128, 32], F32), "fb2": ([128, 8], F32),
    "eye16": ([16, 16], F32), "eye128f": ([128, 128], F32),
    "eye128b": ([128, 128], BF16),
    "selAH": ([16, 16], F32), "hsel": ([24, 3], F32), "spre": ([128, 8], F32),
}
LATE = {"oh16": ([16, 2048], BF16)}

SCRATCH = {
    "nbdA": ([128, 8], F32), "sp8": ([128, 8], F32), "sptmp": ([128, 8], F32),
    "ones_row": ([1, 128], BF16), "ones_col": ([128, 1], BF16),
    "sum24": ([128, 24], F32), "tr24": ([24, 128], F32),
    "selev": ([128, 16], F32), "hcar": ([128, 8], F32), "dcar": ([128, 8], F32),
    "hp": ([128, 8], F32), "hpb": ([128, 8], BF16),
    "mx": ([128, 16], F32),
    "cmb": ([128, 24], F32),
    "sv24": ([128, 24], F32), "cv6": ([128, 8], F32), "cv3": ([128, 4], F32),
}


def build_nc(phase_limit):
    nc = bass.Bass("TRN2", target_bir_lowering=False)
    p = Prog()

    def din(name, shape, dt=F32):
        return nc.dram_tensor(name, shape, dt, kind="ExternalInput").ap()

    x_in = din("x_fm", [D, T])
    out_d = nc.dram_tensor("out_fm", [D, T], F32, kind="ExternalOutput").ap()

    wA_inT = din("wA_inT", [D, 2 * D])
    wA_dT = din("wA_dT", [D, D])
    wA_BT = din("wA_BT", [D, D])
    wA_CT = din("wA_CT", [D, D])
    wA_outT = din("wA_outT", [D, D])
    wB_inT = din("wB_inT", [D, 2 * D])
    wB_gT = din("wB_gT", [D, 2 * D])
    wB_outT = din("wB_outT", [D, D])
    f_w1T = din("f_w1T", [D, 4 * D])
    f_w2T = din("f_w2T", [4 * D, D])
    din_small = {k: din(k, s, d) for k, (s, d) in SMALLS.items()}
    din_oh16 = din("oh16", [16, 2048], BF16)

    agh_in = [nc.dram_tensor(f"agh{i}_in", [3, D], F32) for i in range(2)]
    agh_out = [nc.dram_tensor(f"agh{i}_out", [24, D], F32) for i in range(2)]
    ags_in = [nc.dram_tensor(f"ags{i}_in", [2, D], F32) for i in range(2)]
    ags_out = [nc.dram_tensor(f"ags{i}_out", [16, D], F32) for i in range(2)]

    RES = nc.alloc_sbuf_tensor("RES", [128, 16384], F32).ap()
    TRANS = nc.alloc_sbuf_tensor("TRANS", [128, TRANS_COLS], F32).ap()
    ARB = nc.alloc_sbuf_tensor("ARB", [128, ARB_COLS], BF16).ap()

    t = {k: nc.alloc_sbuf_tensor("t_" + k, s, d).ap() for k, (s, d) in SMALLS.items()}
    t["oh16"] = ARB[0:16, SLC:SLC + 2048]
    for k, (s, d) in SCRATCH.items():
        t[k] = nc.alloc_sbuf_tensor("t_" + k, s, d).ap()
    t["agh2"] = TRANS[0:24, 4096:5120]
    t["ags"] = TRANS[0:16, 0:1024]
    t["ysrow"] = TRANS[0:1, 4096:6144]
    t["yscor"] = TRANS[0:1, 1024:3072]
    t["ysbf"] = ARB[0:1, BC1:BC1 + 2048]
    t["rowseg"] = ARB[0:1, FMF:FMF + 1024]
    t["ltm"] = TRANS[:, 2048:2304]
    t["ltm2"] = TRANS[:, 2304:2560]

    PF = [nc.alloc_psum_tensor(f"pf{i}", [128, 512], F32).ap() for i in range(7)]
    PB7 = nc.alloc_psum_tensor("pb7", [128, 1024], BF16).ap()

    def res_c(cc, c0=0, w=T):
        return RES[:, cc * T + c0: cc * T + c0 + w]

    def arb(base, c0, w):
        return ARB[:, base + c0: base + c0 + w]

    def xnc(cc, c0=0, w=T):
        return arb(XB, cc * T + c0, w)

    def ubuf(cc, c0, w):
        return arb(BB, cc * 2056 + 3 + c0, w)

    def ubuf_raw(cc, c0, w):
        return arb(BB, cc * 2056 + c0, w)

    def dma(eng, out, in_, sem="s_x", noncontig=False, **kw):
        if noncontig:
            def fn(e, o=out, i=in_, kw=kw):
                with nc.allow_non_contiguous_dma("tiny"):
                    return e.dma_start(out=o, in_=i, **kw)
        else:
            def fn(e, o=out, i=in_, kw=kw):
                return e.dma_start(out=o, in_=i, **kw)
        return p.op(eng, fn, inc=sem, n=16)

    def mm(o, l, r, s, tt_):
        return lambda e, o=o, l=l, r=r, s=s, tt_=tt_: e.matmul(o, l, r, start=s, stop=tt_)

    def vcopy(out, in_):
        return lambda e, o=out, i=in_: e.tensor_copy(o, i)

    def act(out, in_, func, bias=None, scale=None):
        kw = {}
        if bias is not None:
            kw["bias"] = bias
        if scale is not None:
            kw["scale"] = scale
        return lambda e, o=out, i=in_, f=func, kw=kw: e.activation(out=o, in_=i, func=f, **kw)

    def tt(out, a, b, op):
        return lambda e, o=out, a=a, b=b, op=op: e.tensor_tensor(o, a, b, op)

    def ts2(out, a, s1, s2, op0, op1):
        return lambda e, o=out, a=a, s1=s1, s2=s2, op0=op0, op1=op1: \
            e.tensor_scalar(o, a, s1, s2, op0, op1)

    def ts1(out, a, s1, op0):
        return lambda e, o=out, a=a, s1=s1, op0=op0: e.tensor_scalar(o, a, s1, None, op0)

    def stt(out, in0, sc, in1, op0, op1):
        return lambda e, o=out, i0=in0, s=sc, i1=in1, op0=op0, op1=op1: \
            e.scalar_tensor_tensor(o, i0, s, i1, op0, op1)

    def scan(out, d0, d1, init, op0, op1):
        return lambda e, o=out, a=d0, b=d1, i=init, op0=op0, op1=op1: \
            e.tensor_tensor_scan(o, a, b, i, op0, op1)

    gemm_banks = Banks(p, [PF[0], PF[1], PF[2], PF[3]])

    def emit_gemm(K, JOs, lhsT_fn, rhs_fn, evac_fn, ngs=NGS, extraK=None, banks=None):
        banks = banks or gemm_banks
        for jo in JOs:
            for (c0, w) in ngs:
                bi, ps, bmark = banks.get()
                p.wait("tensor", bmark)
                mmark = None
                for k in range(K):
                    last = (k == K - 1) and extraK is None
                    f = mm(ps[:, 0:w], lhsT_fn(k, jo), rhs_fn(k, c0, w), k == 0, last)
                    if last:
                        mmark = p.op("tensor", f, inc="s_mm")
                    else:
                        p.op("tensor", f)
                if extraK is not None:
                    mmark = extraK(jo, c0, w, ps)
                emark = evac_fn(jo, c0, w, ps[:, 0:w], mmark)
                banks.set_mark(bi, emark)

    def ln_phase(kind):
        cast_m = []
        for cc in range(CH):
            cm = p.op("vector", vcopy(xnc(cc), res_c(cc)), inc="s_v", nd=cc > 0)
            cast_m.append(cm)
        bcb = Banks(p, [PF[0], PF[1], PF[2]])
        apply_deps = []
        x2segs = [ARB[:, X2A:X2A + 512], ARB[:, X2B:X2B + 512]]
        x2_free = [None, None]
        sumx_bank, sumsq_bank = PF[5], PF[6]
        sx_free = None
        sq_free = None
        for gi, (c0, w) in enumerate(NGS):
            p.wait("tensor", sx_free)
            sxm = None
            for cc in range(CH):
                if gi == 0:
                    p.wait("tensor", cast_m[cc])
                f = mm(sumx_bank[0:1, 0:w], t["ones_col"], xnc(cc, c0, w),
                       cc == 0, cc == CH - 1)
                if cc == CH - 1:
                    sxm = p.op("tensor", f, inc="s_mm")
                else:
                    p.op("tensor", f)
            p.wait("tensor", sq_free)
            sqgm = None
            for cc in range(CH):
                seg = x2segs[cc % 2]
                p.wait("scalar", cast_m[cc])
                p.wait("scalar", x2_free[cc % 2])
                sqm = p.op("scalar", act(seg, xnc(cc, c0, w), AF.Square),
                           inc="s_sc", nd=not (gi == 0 and cc == 0))
                p.wait("tensor", sqm)
                sqgm = p.op("tensor", mm(sumsq_bank[0:1, 0:w], t["ones_col"], seg,
                                         cc == 0, cc == CH - 1), inc="s_mm")
                x2_free[cc % 2] = sqgm
            def ROW(off, w=w):
                return TRANS[0:1, off:off + w]
            p.wait("vector", sxm)
            sx_free = p.op("vector", ts1(ROW(R_MU), sumx_bank[0:1, 0:w],
                                         1.0 / D, OP.mult), inc="s_v")
            p.op("vector", tt(ROW(R_T0), ROW(R_MU), ROW(R_MU), OP.mult))
            p.wait("vector", sqgm)
            p.op("vector", stt(ROW(R_T0), sumsq_bank[0:1, 0:w], 1.0 / D, ROW(R_T0),
                               OP.mult, OP.subtract))
            vm = p.op("vector", ts1(ROW(R_T0), ROW(R_T0), EPS, OP.add), inc="s_v")
            sq_free = vm
            p.wait("scalar", vm)
            p.op("scalar", act(ROW(R_T1), ROW(R_T0), AF.Ln))
            if kind == "plain":
                smk = p.op("scalar", act(ROW(R_R), ROW(R_T1), AF.Exp, scale=-0.5),
                           inc="s_sc")
                p.wait("vector", smk)
                p.op("vector", tt(ROW(R_MR), ROW(R_MU), ROW(R_R), OP.mult))
            else:
                r1m = p.op("scalar", act(ROW(R_R), ROW(R_T1), AF.Exp, scale=-0.5),
                           inc="s_sc")
                p.wait("vector", r1m)
                p.wait("vector", p.cur("s_mm"))
                sgm = p.op("vector", vcopy(t["rowseg"][0:1, 0:w], ROW(R_R)), inc="s_v")
                bi, ps, bmk = bcb.get()
                p.wait("tensor", bmk)
                p.wait("tensor", sgm)
                bmm = p.op("tensor", mm(ps[:, 0:w], t["ones_row"],
                                        t["rowseg"][0:1, 0:w], True, True), inc="s_mm")
                p.wait("vector", bmm)
                r1bc = p.op("vector", vcopy(arb(BB, c0, w), ps[:, 0:w]), inc="s_v")
                bcb.set_mark(bi, r1bc)
                apply_deps.append(r1bc)
                p.op("vector", tt(ROW(R_MR), ROW(R_MU), ROW(R_R), OP.mult))
                p.wait("vector", p.cur("s_mm"))
                sgm2 = p.op("vector", vcopy(t["rowseg"][0:1, 0:w], ROW(R_MR)),
                            inc="s_v")
                bi2, ps2, bmk2 = bcb.get()
                p.wait("tensor", bmk2)
                p.wait("tensor", sgm2)
                bmm2 = p.op("tensor", mm(ps2[:, 0:w], t["ones_row"],
                                         t["rowseg"][0:1, 0:w], True, True),
                            inc="s_mm")
                p.wait("vector", bmm2)
                m1bc = p.op("vector", vcopy(arb(BB, 2048 + c0, w), ps2[:, 0:w]),
                            inc="s_v")
                bcb.set_mark(bi2, m1bc)
                apply_deps.append(m1bc)
                sm2 = p.op("scalar", act(ROW(R_T0), ROW(R_T1), AF.Exp, scale=-1.0),
                           inc="s_sc")
                p.wait("vector", sm2)
                vm2 = p.op("vector", ts2(ROW(R_T0), ROW(R_T0), -EPS, 1.0 + EPS,
                                         OP.mult, OP.add), inc="s_v")
                p.wait("scalar", vm2)
                lm = p.op("scalar", act(ROW(R_T0), ROW(R_T0), AF.Ln), inc="s_sc")
                p.wait("vector", lm)
                vm3 = p.op("vector", tt(ROW(R_T0), ROW(R_T1), ROW(R_T0), OP.add),
                           inc="s_v")
                p.wait("scalar", vm3)
                smk = p.op("scalar", act(ROW(R_R), ROW(R_T0), AF.Exp, scale=-0.5),
                           inc="s_sc")
                p.wait("vector", smk)
                p.op("vector", tt(ROW(R_MR), ROW(R_MU), ROW(R_R), OP.mult))
            p.wait("vector", p.cur("s_mm"))
            c1 = p.op("vector", vcopy(t["rowseg"][0:1, 0:w], ROW(R_R)), inc="s_v")
            c2 = p.op("vector", vcopy(t["rowseg"][0:1, 512:512 + w], ROW(R_MR)),
                      inc="s_v")
            for (src_off, dst_base, dep) in ((0, BC0, c1), (512, BC1, c2)):
                bi, ps, bmk = bcb.get()
                p.wait("tensor", bmk)
                p.wait("tensor", dep)
                bmmk = p.op("tensor", mm(ps[:, 0:w], t["ones_row"],
                                         t["rowseg"][0:1, src_off:src_off + w],
                                         True, True), inc="s_mm")
                p.wait("vector", bmmk)
                cpm = p.op("vector", vcopy(arb(dst_base, c0, w), ps[:, 0:w]),
                           inc="s_v")
                bcb.set_mark(bi, cpm)
                apply_deps.append(cpm)
        stats_done = p.cur("s_mm")
        for mk in apply_deps:
            p.wait("vector", mk)
        p.wait("vector", stats_done)
        apply_m = []
        for cc in range(CH):
            if kind == "double":
                tmpb = arb(BB, 4096, T)
                p.op("vector", tt(tmpb, xnc(cc), arb(BB, 0, T), OP.mult),
                     nd=cc > 0)
                p.op("vector", tt(tmpb, tmpb, arb(BB, 2048, T), OP.subtract))
                p.op("vector", tt(res_c(cc), res_c(cc), tmpb, OP.add))
            p.op("vector", tt(xnc(cc), xnc(cc), arb(BC0, 0, T), OP.mult),
                 nd=(cc > 0 and kind != "double"))
            am = p.op("vector", tt(xnc(cc), xnc(cc), arb(BC1, 0, T), OP.subtract),
                      inc="s_v")
            if kind == "double":
                am = p.op("vector", lambda e, o=xnc(cc), cc=cc:
                          e.tensor_scalar(o, o, t["lng"][:, cc:cc + 1],
                                          t["lnb"][:, cc:cc + 1], OP.mult, OP.add),
                          inc="s_v")
            apply_m.append(am)
        return apply_m

    def emit_ag(idx, nrows, src_mark, halo):
        dram_in = (agh_in if halo else ags_in)[idx]
        dram_out = (agh_out if halo else ags_out)[idx]
        ncols = 8 * nrows
        p.wait("tensor", src_mark)
        p.wait("tensor", p.cur("s_sc"))
        tm = p.op("tensor", lambda e, n=ncols: e.transpose(
            PF[6][0:n, 0:128], t["sum24"][:, 0:n], t["eye128f"]), inc="s_mm")
        p.wait("vector", tm)
        vm = p.op("vector", vcopy(t["tr24"][0:ncols, :], PF[6][0:ncols, 0:128]),
                  inc="s_v")
        p.wait("gpsimd", vm)
        dm = dma("gpsimd", dram_in.ap().rearrange("r (c q) -> (r c) q", q=128),
                 t["tr24"][0:ncols, :], sem="s_g")
        p.wait("gpsimd", dm)
        ccm = p.op("gpsimd", lambda e, i=dram_in, o=dram_out: e.collective_compute(
            "AllGather", OP.bypass, replica_groups=[list(range(NCORES))],
            ins=[i.ap().opt()], outs=[o.ap().opt()]), inc="s_cc")
        dst = t["agh2"] if halo else t["ags"]
        p.wait("sync", ccm)
        rb = dma("sync", dst, dram_out.ap(), sem="s_ag")
        return rb

    def emit_tails_chunk(cc, evac_mark):
        p.wait("vector", evac_mark)
        mk = None
        for r in range(3):
            base = BB + cc * 2056
            p.op("vector", vcopy(t["sum24"][:, r * 8 + cc:r * 8 + cc + 1],
                                 ARB[:, base + 2048 + r:base + 2048 + r + 1]),
                 nd=True)
            mk = p.op("vector", vcopy(t["sv24"][:, r * 8 + cc:r * 8 + cc + 1],
                                      ARB[:, base + 3 + r:base + 3 + r + 1]),
                      inc="s_v", nd=True)
        return mk

    def emit_conv_chunk(cc, wc, bc, evac_mark, par, prev_cm):
        accw = TRANS[:, par * 2048:par * 2048 + 2048]
        p.wait("vector", evac_mark)
        p.wait("vector", prev_cm)
        p.op("vector", lambda e, o=accw, i=ubuf_raw(cc, 0, T),
             s=wc[:, cc * 4:cc * 4 + 1]: e.tensor_scalar(o, i, s, None, OP.mult))
        vm = None
        for k in range(1, 4):
            f = lambda e, o=accw, i=ubuf_raw(cc, k, T), \
                s=wc[:, cc * 4 + k:cc * 4 + k + 1]: \
                e.scalar_tensor_tensor(o, i, s, o, OP.mult, OP.add)
            if k == 3:
                vm = p.op("vector", f, inc="s_v")
            else:
                p.op("vector", f)
        p.wait("scalar", vm)
        cm = p.op("scalar", act(ubuf(cc, 0, T), accw, AF.Identity,
                                bias=bc[:, cc:cc + 1]), inc="s_sc", nd=True)
        return cm

    def emit_conv_fixup(rb_mark, wc, bc):
        p.wait("tensor", rb_mark)
        marks = []
        pm = None
        cm = None
        for cc in range(CH):
            p.wait("tensor", pm)
            tmm = p.op("tensor", mm(PF[6][:, 0:3],
                                    t["agh2"][0:24, cc * 128:(cc + 1) * 128],
                                    t["hsel"], True, True), inc="s_mm")
            p.wait("vector", tmm)
            p.wait("vector", cm)
            p.op("vector", vcopy(t["cv6"][:, 0:3], PF[6][:, 0:3]))
            for r in range(3):
                p.op("vector", vcopy(t["cv6"][:, 3 + r:4 + r],
                                     t["sv24"][:, r * 8 + cc:r * 8 + cc + 1]),
                     nd=True)
            p.op("vector", lambda e, cc=cc: e.tensor_scalar(
                t["cv3"][:, 0:3], t["cv6"][:, 0:3], wc[:, cc * 4:cc * 4 + 1],
                None, OP.mult))
            vm = None
            for k in range(1, 4):
                f = lambda e, k=k, cc=cc: e.scalar_tensor_tensor(
                    t["cv3"][:, 0:3], t["cv6"][:, k:k + 3],
                    wc[:, cc * 4 + k:cc * 4 + k + 1], t["cv3"][:, 0:3],
                    OP.mult, OP.add)
                if k == 3:
                    vm = p.op("vector", f, inc="s_v")
                else:
                    p.op("vector", f)
            pm = vm
            p.wait("scalar", vm)
            cm = p.op("scalar", act(ubuf(cc, 0, 3), t["cv3"][:, 0:3], AF.Identity,
                                    bias=bc[:, cc:cc + 1]), inc="s_sc")
            marks.append(cm)
        return marks

    def emit_combine(rb_mark):
        p.wait("tensor", rb_mark)
        marks = []
        pm = None
        for cc in range(CH):
            p.wait("tensor", pm)
            tmm = p.op("tensor", mm(PF[6][:, 0:16],
                                    t["ags"][0:16, cc * 128:(cc + 1) * 128],
                                    t["selAH"], True, True), inc="s_mm")
            p.wait("vector", tmm)
            pm = p.op("vector", vcopy(t["selev"], PF[6][:, 0:16]), inc="s_v")
            TA, TH, TS = t["cmb"][:, 0:8], t["cmb"][:, 8:16], t["cmb"][:, 16:24]
            p.op("vector", ts1(TA, t["selev"][:, 0:8], -1.0, OP.add))
            p.op("vector", tt(TA, TA, t["spre"], OP.mult))
            p.op("vector", ts1(TA, TA, 1.0, OP.add))
            p.op("vector", tt(TH, t["selev"][:, 8:16], t["spre"], OP.mult))
            p.op("vector", scan(TS, TA, TH, 0.0, OP.mult, OP.add))
            cm = p.op("vector", vcopy(t["hp"][:, cc:cc + 1], TS[:, 7:8]), inc="s_v")
            marks.append(cm)
        return marks

    for qi, eng in enumerate(("sync", "scalar", "gpsimd")):
        lo = [0, 3, 6][qi]
        hi = [3, 6, 8][qi]
        dma(eng, RES[:, lo * T:hi * T].rearrange("p (c q) -> p c q", q=T),
            x_in[lo * 128:hi * 128, :].rearrange("(c p) q -> p c q", p=128),
            sem="s_x")
    for k in din_small:
        dma("sync", t[k], din_small[k], sem="s_x")
    ld0 = p.cur("s_x")
    p.wait("vector", ld0)
    p.op("vector", lambda e: e.memset(t["ones_row"], 1.0))
    p.op("vector", lambda e: e.memset(t["ones_col"], 1.0))
    for cc in range(CH):
        p.op("vector", lambda e, a=ARB[:, BB + cc * 2056:BB + cc * 2056 + 3]:
             e.memset(a, 0.0), nd=True)
    p.op("vector", ts1(t["nbdA"], t["bdA"], -1.0, OP.mult))
    rl = p.op("vector", ts1(t["sp8"], t["fb"], 0.0, OP.max), inc="s_v")
    p.wait("scalar", rl)
    p.op("scalar", act(t["sptmp"], t["fb"], AF.Abs))
    p.op("scalar", act(t["sptmp"], t["sptmp"], AF.Exp, scale=-1.0))
    lm0 = p.op("scalar", act(t["sptmp"], t["sptmp"], AF.Ln, bias=1.0), inc="s_sc")
    p.wait("vector", lm0)
    p.op("vector", tt(t["sp8"], t["sp8"], t["sptmp"], OP.add))
    p.op("vector", ts1(t["sp8"], t["sp8"], -8.0, OP.mult))
    WINU = D2B
    for cc in range(CH):
        dma("gpsimd", arb(WINU, cc * 1024, 1024),
            wA_inT[cc * 128:(cc + 1) * 128, 0:1024], sem="s_w")
    wu_mark = p.cur("s_w")

    def dump_bf(base, stride, off):
        p.wait("gpsimd", p.cur("s_v"))
        p.wait("gpsimd", p.cur("s_sc"))
        for cc in range(CH):
            dma("gpsimd", out_d[cc * 128:(cc + 1) * 128, :],
                ARB[:, base + cc * stride + off: base + cc * stride + off + T],
                sem="s_st")
        p.wait("sync", p.cur("s_st"))

    def dump_f32row(ap_row):
        p.wait("sync", p.cur("s_v"))
        dma("sync", out_d[0:1, :], ap_row, sem="s_st")
        p.wait("sync", p.cur("s_st"))

    def attn_a():
        apply_m = ln_phase("double")
        if DBG in ("bc0", "bc1"):
            dump_bf(BC0 if DBG == "bc0" else BC1, 0, 0)
            return None
        if DBG == "xn":
            for mk in apply_m:
                p.wait("gpsimd", mk)
            dump_bf(XB, T, 0)
            return None
        for mk in apply_m:
            p.wait("tensor", mk)
        p.wait("tensor", wu_mark)

        evu_n = [0]

        def ev_u(jo, c0, w, ps, mmark):
            p.wait("scalar", mmark)
            evu_n[0] += 1
            return p.op("scalar", act(ubuf(jo, c0, w), ps, AF.Identity,
                                      bias=t["binA"][:, jo:jo + 1]), inc="s_sc",
                        nd=evu_n[0] > 1)
        prev_cm = [None, None]
        tails_mk = None
        for jo in range(8):
            emit_gemm(8, [jo],
                      lambda k, jjo: arb(WINU, k * 1024 + jjo * 128, 128),
                      lambda k, c0, w: xnc(k, c0, w), ev_u)
            evm = p.cur("s_sc")
            if DBG == "u":
                continue
            tails_mk = emit_tails_chunk(jo, evm)
            prev_cm[jo % 2] = emit_conv_chunk(jo, t["wcA"], t["bcA"], evm,
                                              jo % 2, prev_cm[jo % 2])
        if DBG == "u":
            dump_bf(BB, 2056, 3)
            return None
        rb1 = emit_ag(0, 3, tails_mk, True)
        halo_m = emit_conv_fixup(rb1, t["wcA"], t["bcA"])
        conv_m = [p.cur("s_sc")]
        if DBG == "uc":
            for cm in conv_m:
                p.wait("gpsimd", cm)
            dump_bf(BB, 2056, 3)
            return None

        zbanks = Banks(p, [PF[0], PF[1], PF[6]])
        zbanks.marks[2] = halo_m[-1]
        ys_banks = [PF[2], PF[3], PF[4], PF[5]]
        ys_marks = [None] * 4
        prev_v_par = [None, None]
        A_P = (TR_A, TR_A2)
        B_P = (TR_B, TR_B2)
        C_P = (MS1, MS2)
        MS0_P = (MS0, SLC + 3072)
        ys_guard = [None, None]
        pend = [None]

        def flush_ys():
            if pend[0] is None:
                return
            pcc, phalf, ppar, phcm = pend[0]
            pend[0] = None
            for sub in range(2):
                gidx = phalf * 2 + sub
                p.wait("tensor", phcm)
                f = mm(ys_banks[gidx][0:1, 0:512], t["ones_col"],
                       arb(MS0_P[ppar], sub * 512, 512), pcc == 0, pcc == CH - 1)
                m = p.op("tensor", f, inc="s_mm")
                ys_guard[ppar] = m
                if pcc == CH - 1:
                    ys_marks[gidx] = m
        DBCR = [SLC + i * 1024 for i in range(3)]
        dbc_ring_guard = [None] * 3
        dbc_slice_m = [None] * 24
        ln_apply_v = p.cur("s_v")
        dbc_srcs = (wA_dT, wA_BT, wA_CT)

        def load_dbc(s):
            cc2, pi2 = s // 3, s % 3
            ri = s % 3
            p.wait("gpsimd", dbc_ring_guard[ri])
            if s < 3:
                p.wait("gpsimd", ln_apply_v)
            dma("gpsimd", arb(DBCR[ri], 0, 1024).rearrange(
                "p (c q) -> p c q", q=128),
                dbc_srcs[pi2][:, cc2 * 128:(cc2 + 1) * 128].rearrange(
                    "(c p) q -> p c q", p=128), sem=f"s_d{ri}")
            dbc_slice_m[s] = p.cur(f"s_d{ri}")
        for s in range(3):
            load_dbc(s)
        for cc in range(CH):
            if cc == 0:
                for cm in conv_m:
                    p.wait("tensor", cm)
            for half in range(2):
                hs = half * 1024
                par = (cc * 2 + half) % 2
                TRA = TRANS[:, A_P[par]:A_P[par] + 1024]
                TRB = TRANS[:, B_P[par]:B_P[par] + 1024]
                CSL = arb(C_P[par], 0, 1024)
                evC = None
                first_ev = True
                for pi in range(3):
                    s = cc * 3 + pi
                    if half == 0:
                        p.wait("tensor", dbc_slice_m[s])
                    for sub in range(2):
                        c0 = hs + sub * 512
                        bi, ps, bmk = zbanks.get()
                        p.wait("tensor", bmk)
                        mmark = None
                        for k in range(CH):
                            f = mm(ps[:, 0:512],
                                   arb(DBCR[s % 3], k * 128, 128),
                                   ubuf(k, c0, 512), k == 0, k == CH - 1)
                            if k == CH - 1:
                                mmark = p.op("tensor", f, inc="s_mm")
                            else:
                                p.op("tensor", f)
                        p.wait("scalar", mmark)
                        p.wait("scalar", prev_v_par[par])
                        off = sub * 512
                        if pi == 0:
                            emk = p.op("scalar", act(
                                TRA[:, off:off + 512], ps[:, 0:512],
                                AF.Sigmoid, bias=t["nbdA"][:, cc:cc + 1], scale=-1.0),
                                inc="s_sc", nd=not first_ev)
                        elif pi == 1:
                            emk = p.op("scalar", act(
                                TRB[:, off:off + 512], ps[:, 0:512],
                                AF.Identity, bias=t["bBA"][:, cc:cc + 1]),
                                inc="s_sc", nd=True)
                        else:
                            emk = p.op("scalar", act(
                                CSL[:, off:off + 512], ps[:, 0:512],
                                AF.Identity, bias=t["bCA"][:, cc:cc + 1]),
                                inc="s_sc", nd=True)
                            evC = emk
                        first_ev = False
                        zbanks.set_mark(bi, emk)
                    if half == 1:
                        dbc_ring_guard[s % 3] = p.cur("s_mm")
                        if s + 3 < 24:
                            load_dbc(s + 3)
                flush_ys()
                MS0B = arb(MS0_P[par], 0, 1024)
                p.wait("vector", evC)
                p.wait("vector", ys_guard[par])
                p.op("vector", tt(MS0B, ubuf(cc, hs, 1024),
                                  TRB, OP.mult))
                init_h = 0.0 if half == 0 else t["hcar"][:, cc:cc + 1]
                init_d = 1.0 if half == 0 else t["dcar"][:, cc:cc + 1]
                p.op("vector", scan(TRANS[:, TR_H:TR_H + 1024], TRA,
                                    MS0B, init_h, OP.mult, OP.add))
                p.op("vector", scan(TRANS[:, TR_C:TR_C + 1024], TRA, TRA, init_d,
                                    OP.mult, OP.bypass), nd=True)
                hcm = p.op("vector", tt(MS0B,
                                        TRANS[:, TR_H:TR_H + 1024],
                                        CSL, OP.mult), inc="s_v")
                pend[0] = (cc, half, par, hcm)
                p.op("vector", tt(arb(D2B, cc * 2048 + hs, 1024),
                                  TRANS[:, TR_C:TR_C + 1024],
                                  CSL, OP.mult), nd=True)
                p.op("vector", vcopy(t["hcar"][:, cc:cc + 1],
                                     TRANS[:, TR_H + 1023:TR_H + 1024]), nd=True)
                prev_v_par[par] = p.op("vector", vcopy(
                    t["dcar"][:, cc:cc + 1], TRANS[:, TR_C + 1023:TR_C + 1024]),
                    inc="s_v", nd=True)
        flush_ys()
        d2_all = p.cur("s_v")
        if DBG == "d2":
            dump_bf(D2B, 2048, 0)
            return None
        ysev = []
        for gidx in range(4):
            p.wait("vector", ys_marks[gidx])
            ev = p.op("vector", vcopy(t["ysrow"][0:1, gidx * 512:(gidx + 1) * 512],
                                      ys_banks[gidx][0:1, 0:512]), inc="s_v")
            ysev.append(ev)
        if DBG == "ysl":
            dump_f32row(t["ysrow"])
            return None
        p.op("vector", vcopy(t["sum24"][:, 0:8], t["dcar"]))
        sm = p.op("vector", vcopy(t["sum24"][:, 8:16], t["hcar"]), inc="s_v")
        rb2 = emit_ag(0, 2, sm, False)
        if DBG == "ags":
            p.wait("sync", rb2)
            p.wait("sync", p.cur("s_v"))
            stm_ = dma("sync", out_d[0:16, 0:1024], t["ags"], sem="s_st")
            stm_ = dma("sync", out_d[16:144, 1024:1048], t["sum24"], sem="s_st")
            p.wait("sync", stm_)
            return None
        cmb_m = emit_combine(rb2)
        for mk in cmb_m:
            p.wait("vector", mk)
        hpb_m = p.op("vector", vcopy(t["hpb"], t["hp"]), inc="s_v")
        if DBG == "hp":
            p.wait("sync", p.cur("s_v"))
            dma("sync", out_d[0:128, 0:8], t["hp"], sem="s_st")
            dma("sync", out_d[0:128, 8:16], t["spre"], sem="s_st")
            dma("sync", out_d[0:128, 16:40], t["cmb"], sem="s_st")
            stm_ = dma("sync", out_d[0:128, 40:56], t["selev"], sem="s_st")
            p.wait("sync", stm_)
            return None
        p.wait("tensor", hpb_m)
        p.wait("tensor", d2_all)
        yc = Banks(p, [PF[0], PF[1]])
        for gi, (c0, w) in enumerate(NGS):
            bi, ps, bmk = yc.get()
            p.wait("tensor", bmk)
            mmark = None
            for cc in range(CH):
                f = mm(ps[0:1, 0:w], t["hpb"][:, cc:cc + 1],
                       arb(D2B, cc * 2048 + c0, w), cc == 0, cc == CH - 1)
                if cc == CH - 1:
                    mmark = p.op("tensor", f, inc="s_mm")
                else:
                    p.op("tensor", f)
            p.wait("vector", mmark)
            ev = p.op("vector", vcopy(t["yscor"][0:1, c0:c0 + w], ps[0:1, 0:w]),
                      inc="s_v")
            yc.set_mark(bi, ev)
        for mk in ysev:
            p.wait("vector", mk)
        p.op("vector", tt(t["ysrow"], t["ysrow"], t["yscor"], OP.add))
        ysb = p.op("vector", vcopy(t["ysbf"], t["ysrow"]), inc="s_v")
        if DBG == "ys":
            dump_f32row(t["ysrow"])
            return None
        ysbc = Banks(p, [PF[0], PF[1]])
        ysbc_m = []
        for gi, (c0, w) in enumerate(NGS):
            bi, ps, bmk = ysbc.get()
            p.wait("tensor", bmk)
            p.wait("tensor", ysb)
            bmmk = p.op("tensor", mm(ps[:, 0:w], t["ones_row"],
                                     t["ysbf"][0:1, c0:c0 + w], True, True),
                        inc="s_mm")
            p.wait("vector", bmmk)
            cpm = p.op("vector", vcopy(arb(SLC, c0, w), ps[:, 0:w]), inc="s_v")
            ysbc.set_mark(bi, cpm)
            ysbc_m.append(cpm)
        ycor_mm = p.cur("s_mm")
        p.wait("gpsimd", ycor_mm)
        for cc in range(CH):
            dma("gpsimd", arb(D2B, cc * 1024, 1024),
                wA_inT[cc * 128:(cc + 1) * 128, 1024:2048], sem="s_w")
        for cc in range(CH):
            dma("gpsimd", arb(D2B, 8192 + cc * 1024, 1024),
                wA_outT[cc * 128:(cc + 1) * 128, :], sem="s_w")
        wv_mark = p.cur("s_w")
        p.wait("tensor", wv_mark)
        for mk in ysbc_m:
            p.wait("vector", mk)
            p.wait("tensor", mk)
        vtmp = [arb(MS0, 0, 512), arb(MS1, 0, 512)]
        vstate = [None, None]
        vpi = [0]

        def ev_v(jo, c0, w, ps, mmark):
            pi = vpi[0]
            vpi[0] = 1 - pi
            p.wait("scalar", mmark)
            p.wait("scalar", vstate[pi])
            am = p.op("scalar", act(vtmp[pi], ps, AF.Sigmoid,
                                    bias=t["binA"][:, 8 + jo:9 + jo]), inc="s_sc",
                      nd=vstate[pi] is not None)
            p.wait("vector", am)
            vm = p.op("vector", tt(ubuf(jo, c0, w), vtmp[pi],
                                   arb(SLC, c0, w), OP.mult), inc="s_v",
                      nd=vstate[pi] is not None)
            vstate[pi] = vm
            return vm
        emit_gemm(8, range(8),
                  lambda k, jo: arb(D2B, k * 1024 + jo * 128, 128),
                  lambda k, c0, w: xnc(k, c0, w), ev_v)
        y_done = p.cur("s_v")
        if DBG == "y":
            dump_bf(BB, 2056, 3)
            return None
        p.wait("tensor", y_done)

        evo_n = [0]

        def ev_o(jo, c0, w, ps, mmark):
            p.wait("vector", mmark)
            evo_n[0] += 1
            return p.op("vector", stt(res_c(jo, c0, w), ps,
                                      t["boA"][:, jo:jo + 1], res_c(jo, c0, w),
                                      OP.add, OP.add), inc="s_v", nd=evo_n[0] > 1)
        emit_gemm(8, range(8),
                  lambda k, jo: arb(D2B, 8192 + k * 1024 + jo * 128, 128),
                  lambda k, c0, w: ubuf(k, c0, w), ev_o)
        return p.cur("s_v"), p.cur("s_mm")

    def ffn(prev_v, prev_mm):
        W1R = [FWC, FWA, FWB, FWC]
        W2R = [FWB, FWC, FWA, FWB]

        def load_w1(k, waitmark):
            p.wait("gpsimd", waitmark)
            sname = "s_f1" if k % 2 == 0 else "s_f1b"
            for cc in range(CH):
                dma("gpsimd", arb(W1R[k], cc * 1024, 1024),
                    f_w1T[cc * 128:(cc + 1) * 128, k * 1024:(k + 1) * 1024],
                    sem=sname)
            return p.cur(sname)

        def load_w2(k, waitmark):
            p.wait("gpsimd", waitmark)
            dma("gpsimd", ARB[:, W2R[k]:W2R[k] + 8192].rearrange(
                "p (c q) -> p c q", q=1024),
                f_w2T[k * 1024:(k + 1) * 1024, :].rearrange(
                    "(c p) q -> p c q", p=128),
                sem="s_f2" if k % 2 == 0 else "s_f2b")
            return p.cur("s_f2" if k % 2 == 0 else "s_f2b")

        w1m = [None] * 4
        w2m = [None] * 4
        p.wait("gpsimd", prev_v)
        p.wait("gpsimd", prev_mm)
        w1m[0] = load_w1(0, None)
        w2m[0] = load_w2(0, None)

        apply_m = ln_phase("plain")
        p.wait("gpsimd", p.cur("s_v"))
        oh_m = dma("gpsimd", t["oh16"], din_oh16, sem="s_fo")
        p.wait("tensor", oh_m)
        for mk in apply_m:
            p.wait("tensor", mk)
        rt = Banks(p, [PF[0], PF[1]])
        lg_m = []
        for gi, (c0, w) in enumerate(NGS):
            bi, ps, bmk = rt.get()
            p.wait("tensor", bmk)
            mmark = None
            for k in range(CH):
                f = mm(ps[0:16, 0:w], t["rT"][:, k * 16:(k + 1) * 16],
                       xnc(k, c0, w), k == 0, k == CH - 1)
                if k == CH - 1:
                    mmark = p.op("tensor", f, inc="s_mm")
                else:
                    p.op("tensor", f)
            p.wait("vector", mmark)
            ev = p.op("vector", lambda e, o=TRANS[0:16, c0:c0 + w], i=ps[0:16, 0:w]:
                      e.tensor_scalar(o, i, t["rb"], None, OP.add), inc="s_v")
            rt.set_mark(bi, ev)
            lg_m.append(ev)
        for mk in lg_m:
            p.wait("tensor", mk)
        tms = None
        for g in range(16):
            tms = p.op("tensor", lambda e, g=g: e.transpose(
                PF[6][:, g * 16:(g + 1) * 16],
                TRANS[0:16, g * 128:(g + 1) * 128], t["eye16"]), inc="s_mm")
        p.wait("vector", tms)
        p.op("vector", vcopy(t["ltm"], PF[6][:, 0:256]))
        cur, other = t["ltm"], t["ltm2"]
        for r in range(4):
            p.op("vector", lambda e, i=cur: e.tensor_reduce(
                t["mx"], i.rearrange("p (g j) -> p g j", j=16), AX.X, OP.max))
            p.op("vector", lambda e, o=other[:, 0:128], m=t["mx"][:, 0:8],
                 i=cur[:, 0:128]: e.match_replace(o, m, i, NEG))
            p.op("vector", lambda e, o=other[:, 128:256], m=t["mx"][:, 8:16],
                 i=cur[:, 128:256]: e.match_replace(o, m, i, NEG))
            cur, other = other, cur
        mkm = p.op("vector", lambda e, i=cur: e.tensor_scalar(
            arb(FTM, 0, 256), i, NEG, None, OP.is_equal), inc="s_v")
        mask_done = None
        for wv in range(2):
            p.wait("tensor", mkm)
            if wv == 1:
                p.wait("tensor", mask_done)
            tbm = None
            for g in range(8):
                gg = wv * 8 + g
                tbm = p.op("tensor", lambda e, g=g, gg=gg: e.transpose(
                    PB7[0:16, g * 128:(g + 1) * 128],
                    arb(FTM, gg * 16, 16), t["eye128b"]), inc="s_mm")
            p.wait("vector", tbm)
            mask_done = p.op("vector", vcopy(
                ARB[0:16, FMF + wv * 1024:FMF + wv * 1024 + 1024],
                PB7[0:16, 0:1024]), inc="s_v")
        hb = Banks(p, [PF[0], PF[1], PF[6]])
        yb = Banks(p, [PF[2], PF[3]])
        mb = Banks(p, [PF[4], PF[5]])
        last_h_mm = [None] * 4
        last_y_mm = None
        prev_ti_y = None
        MBP = (FMB, FMB2)
        mask_guard = [None, None]

        def emit_mask_build(k, ti, par):
            marks = []
            p.wait("scalar", mask_guard[par])
            for b4 in range(4):
                b = 4 * k + b4
                bi, ps, bmk = mb.get()
                p.wait("tensor", bmk)
                mmk = p.op("tensor", mm(
                    ps[:, 0:512], t["oh16"][0:16, b * 128:(b + 1) * 128],
                    ARB[0:16, FMF + ti * 512:FMF + ti * 512 + 512], True, True),
                    inc="s_mm")
                p.wait("scalar", mmk)
                ev = p.op("scalar", act(arb(MBP[par], b4 * 512, 512),
                                        ps[:, 0:512], AF.Identity), inc="s_sc",
                          nd=(b4 > 0))
                mb.set_mark(bi, ev)
                marks.append(ev)
            return marks

        p.wait("tensor", mask_done)
        p.wait("tensor", oh_m)
        mpar = 0
        mbm = emit_mask_build(0, 0, 0)
        for k in range(4):
            if k + 1 < 4:
                w1m[k + 1] = load_w1(k + 1, last_y_mm)
            for ti in range(4):
                c0 = ti * 512
                cur_par = mpar
                p.wait("tensor", w1m[k])
                hm = None
                for sub in range(8):
                    bi, ps, bmk = hb.get()
                    p.wait("tensor", bmk)
                    mmark = None
                    for kc in range(CH):
                        f = mm(ps[:, 0:512], arb(W1R[k], kc * 1024 + sub * 128, 128),
                               xnc(kc, c0, 512), kc == 0, kc == CH - 1)
                        if kc == CH - 1:
                            mmark = p.op("tensor", f, inc="s_mm")
                        else:
                            p.op("tensor", f)
                    last_h_mm[k] = mmark
                    p.wait("scalar", mmark)
                    p.wait("scalar", prev_ti_y)
                    am = p.op("scalar", act(
                        arb(FH, sub * 512, 512), ps[:, 0:512], AF.Relu,
                        bias=t["fb1"][:, k * 8 + sub:k * 8 + sub + 1]), inc="s_sc",
                        nd=(sub > 0))
                    p.wait("vector", am)
                    for mk in mbm:
                        p.wait("vector", mk)
                    hm = p.op("vector", tt(arb(FH, sub * 512, 512),
                                           arb(FH, sub * 512, 512),
                                           arb(MBP[cur_par], (sub // 2) * 512, 512),
                                           OP.mult),
                              inc="s_v", nd=(sub > 0))
                    hb.set_mark(bi, am)
                mask_guard[cur_par] = hm
                p.wait("tensor", w2m[k])
                p.wait("tensor", hm)
                nk, nti = (k, ti + 1) if ti + 1 < 4 else (k + 1, 0)
                built_next = False
                for jo in range(8):
                    bi, ps, bmk = yb.get()
                    p.wait("tensor", bmk)
                    mmark = None
                    for sub in range(CH):
                        f = mm(ps[:, 0:512], arb(W2R[k], sub * 1024 + jo * 128, 128),
                               arb(FH, sub * 512, 512), sub == 0, sub == CH - 1)
                        if sub == CH - 1:
                            mmark = p.op("tensor", f, inc="s_mm")
                        else:
                            p.op("tensor", f)
                    last_y_mm = mmark
                    p.wait("vector", mmark)
                    if k == 0:
                        ev = p.op("vector", stt(res_c(jo, c0, 512), ps[:, 0:512],
                                                t["fb2"][:, jo:jo + 1],
                                                res_c(jo, c0, 512), OP.add, OP.add),
                                  inc="s_v", nd=(jo > 0))
                    else:
                        ev = p.op("vector", tt(res_c(jo, c0, 512), ps[:, 0:512],
                                               res_c(jo, c0, 512), OP.add),
                                  inc="s_v", nd=(jo > 0))
                    yb.set_mark(bi, ev)
                    if jo == 1 and nk < 4:
                        mbm = emit_mask_build(nk, nti, 1 - cur_par)
                        built_next = True
                if not built_next and nk < 4:
                    mbm = emit_mask_build(nk, nti, 1 - cur_par)
                mpar = 1 - cur_par
                prev_ti_y = last_y_mm
            if k + 1 < 4:
                w2m[k + 1] = load_w2(k + 1, last_h_mm[k])
        return p.cur("s_v"), p.cur("s_mm")

    def attn_b(prev_v, prev_mm):
        GATE = D2B
        apply_m = ln_phase("plain")
        ln_v = p.cur("s_v")
        for mk in apply_m:
            p.wait("tensor", mk)
        slc_bufs = [SLC, SLC + 1024]
        slice_mm = [None, None]
        prev_cm = [None, None]
        tails_mk = None
        for jo in range(16):
            pi = jo % 2
            p.wait("gpsimd", slice_mm[pi])
            if jo == 0:
                p.wait("gpsimd", prev_v)
                p.wait("gpsimd", prev_mm)
            if jo == 1:
                p.wait("gpsimd", ln_v)
            dma("gpsimd", arb(slc_bufs[pi], 0, 1024).rearrange(
                "p (c q) -> p c q", q=128),
                wB_inT[:, jo * 128:(jo + 1) * 128].rearrange(
                    "(c p) q -> p c q", p=128), sem=f"s_p{pi}")
            sld = p.cur(f"s_p{pi}")
            p.wait("tensor", sld)

            def ev_p(jjo, c0, w, ps, mmark, jo=jo):
                p.wait("scalar", mmark)
                if jo < 8:
                    return p.op("scalar", act(arb(GATE, jo * 2048 + c0, w), ps,
                                              AF.Gelu), inc="s_sc", nd=jo + c0 > 0)
                return p.op("scalar", act(ubuf(jo - 8, c0, w), ps, AF.Identity),
                            inc="s_sc", nd=True)
            emit_gemm(8, [jo],
                      lambda k, jjo, pi=pi: arb(slc_bufs[pi], k * 128, 128),
                      lambda k, c0, w: xnc(k, c0, w), ev_p)
            slice_mm[pi] = p.cur("s_mm")
            if jo >= 8:
                evm = p.cur("s_sc")
                tails_mk = emit_tails_chunk(jo - 8, evm)
                prev_cm[(jo - 8) % 2] = emit_conv_chunk(
                    jo - 8, t["wcB"], t["bcB"], evm, (jo - 8) % 2,
                    prev_cm[(jo - 8) % 2])
        rb3 = emit_ag(1, 3, tails_mk, True)
        halo_m = emit_conv_fixup(rb3, t["wcB"], t["bcB"])
        conv_m = [p.cur("s_sc")]
        gb = Banks(p, [PF[0], PF[1], PF[6]])
        gb.marks[2] = halo_m[-1]
        gslice = [SLC, SLC + 2048]
        slice_guard = [None, None]
        prev_cc_mm = None
        HB = (TR_A, TR_A2)
        LB = (TR_B, TR_B2)
        CB = (TR_C, TR_C2)
        IB = (MS2, MS0)
        traf_guard = [None, None]
        trb_guard = [None, None]
        sgi_guard = [None, None]
        for cc in range(CH):
            pi = cc % 2
            p.wait("gpsimd", slice_guard[pi])
            for gk in range(2):
                dma("gpsimd", arb(gslice[pi], gk * 1024, 1024).rearrange(
                    "p (c q) -> p c q", q=128),
                    wB_gT[:, gk * 1024 + cc * 128: gk * 1024 + (cc + 1) * 128]
                    .rearrange("(c p) q -> p c q", p=128), sem=f"s_b{pi}")
            gld = p.cur(f"s_b{pi}")
            p.wait("tensor", gld)
            if cc == 0:
                for cm in conv_m:
                    p.wait("tensor", cm)
            ev_f = [None, None]
            ev_i = [None, None]
            first_ev = cc == 0
            for half in range(2):
                hs = half * 1024
                TRA = TRANS[:, HB[half]:HB[half] + 1024]
                SGI = arb(IB[half], 0, 1024)
                for gk in range(2):
                    for sub in range(2):
                        c0 = hs + sub * 512
                        bi, ps, bmk = gb.get()
                        p.wait("tensor", bmk)
                        mmark = None
                        for k in range(CH):
                            f = mm(ps[:, 0:512],
                                   arb(gslice[pi], gk * 1024 + k * 128, 128),
                                   ubuf(k, c0, 512), k == 0, k == CH - 1)
                            if k == CH - 1:
                                mmark = p.op("tensor", f, inc="s_mm")
                            else:
                                p.op("tensor", f)
                        p.wait("scalar", mmark)
                        off = sub * 512
                        if gk == 0:
                            p.wait("scalar", traf_guard[half])
                            ev_f[half] = p.op("scalar", act(
                                TRA[:, off:off + 512], ps[:, 0:512],
                                AF.Sigmoid, bias=t["bgB"][:, cc:cc + 1]),
                                inc="s_sc", nd=not first_ev)
                            gb.set_mark(bi, ev_f[half])
                        else:
                            p.wait("scalar", sgi_guard[half])
                            ev_i[half] = p.op("scalar", act(
                                SGI[:, off:off + 512], ps[:, 0:512],
                                AF.Sigmoid, bias=t["bgB"][:, 8 + cc:9 + cc]),
                                inc="s_sc", nd=True)
                            gb.set_mark(bi, ev_i[half])
                        first_ev = False
            prev_cc_mm = p.cur("s_mm")
            slice_guard[pi] = prev_cc_mm
            TRA0, TRA1 = (TRANS[:, HB[h]:HB[h] + 1024] for h in (0, 1))
            TRB0, TRB1 = (TRANS[:, LB[h]:LB[h] + 1024] for h in (0, 1))
            TRC0, TRC1 = (TRANS[:, CB[h]:CB[h] + 1024] for h in (0, 1))
            SGI0, SGI1 = (arb(IB[h], 0, 1024) for h in (0, 1))
            TRH = TRANS[:, TR_H:TR_H + 1024]
            p.wait("vector", ev_f[0])
            p.wait("vector", trb_guard[0])
            lam0 = p.op("vector", lambda e, o=TRB0, i=TRA0, cc=cc:
                        e.tensor_scalar(o, i, t["sp8"][:, cc:cc + 1], None,
                                        OP.mult), inc="s_v", nd=True)
            p.wait("vector", ev_f[1])
            p.wait("vector", trb_guard[1])
            lam1 = p.op("vector", lambda e, o=TRB1, i=TRA1, cc=cc:
                        e.tensor_scalar(o, i, t["sp8"][:, cc:cc + 1], None,
                                        OP.mult), inc="s_v", nd=True)
            p.wait("scalar", lam0)
            p.op("scalar", act(TRA0, TRB0, AF.Exp))
            p.wait("scalar", lam1)
            p.op("scalar", act(TRA1, TRB1, AF.Exp), nd=True)
            sq0 = p.op("scalar", act(TRB0, TRA0, AF.Square), inc="s_sc")
            sq1 = p.op("scalar", act(TRB1, TRA1, AF.Square), inc="s_sc",
                       nd=True)
            p.wait("vector", sq0)
            b2m0 = p.op("vector", ts2(TRB0, TRB0, -1.0, 1.0 + 1e-6,
                                      OP.mult, OP.add), inc="s_v")
            p.wait("vector", sq1)
            b2m1 = p.op("vector", ts2(TRB1, TRB1, -1.0, 1.0 + 1e-6,
                                      OP.mult, OP.add), inc="s_v")
            p.wait("scalar", b2m0)
            btm0 = p.op("scalar", act(TRC0, TRB0, AF.Sqrt), inc="s_sc")
            p.wait("scalar", b2m1)
            btm1 = p.op("scalar", act(TRC1, TRB1, AF.Sqrt), inc="s_sc",
                        nd=True)
            trb_guard[0], trb_guard[1] = btm0, btm1
            p.wait("vector", btm0)
            p.wait("vector", ev_i[0])
            p.op("vector", tt(SGI0, TRC0, SGI0, OP.mult), nd=True)
            p.op("vector", tt(SGI0, SGI0, ubuf(cc, 0, 1024), OP.mult))
            sc1_0 = p.op("vector", scan(TRH, TRA0, SGI0, 0.0,
                                        OP.mult, OP.add), inc="s_v")
            sc2_0 = p.op("vector", scan(TRC0, TRA0, TRA0, 1.0,
                                        OP.mult, OP.bypass), inc="s_v", nd=True)
            p.op("vector", vcopy(t["hcar"][:, cc:cc + 1],
                                 TRH[:, 1023:1024]))
            p.op("vector", vcopy(t["dcar"][:, cc:cc + 1],
                                 TRC0[:, 1023:1024]), nd=True)
            p.op("vector", tt(arb(XB, cc * 2048, 1024),
                              arb(GATE, cc * 2048, 1024), TRC0, OP.mult),
                 nd=True)
            p.op("vector", tt(arb(GATE, cc * 2048, 1024),
                              arb(GATE, cc * 2048, 1024), TRH, OP.mult),
                 nd=True)
            p.wait("vector", btm1)
            p.wait("vector", ev_i[1])
            p.op("vector", tt(SGI1, TRC1, SGI1, OP.mult), nd=True)
            p.op("vector", tt(SGI1, SGI1, ubuf(cc, 1024, 1024), OP.mult))
            sc1_1 = p.op("vector", scan(TRH, TRA1, SGI1,
                                        t["hcar"][:, cc:cc + 1],
                                        OP.mult, OP.add), inc="s_v")
            sc2_1 = p.op("vector", scan(TRC1, TRA1, TRA1,
                                        t["dcar"][:, cc:cc + 1],
                                        OP.mult, OP.bypass), inc="s_v", nd=True)
            p.op("vector", vcopy(t["hcar"][:, cc:cc + 1],
                                 TRH[:, 1023:1024]))
            p.op("vector", vcopy(t["dcar"][:, cc:cc + 1],
                                 TRC1[:, 1023:1024]), nd=True)
            p.op("vector", tt(arb(XB, cc * 2048 + 1024, 1024),
                              arb(GATE, cc * 2048 + 1024, 1024), TRC1, OP.mult),
                 nd=True)
            p.op("vector", tt(arb(GATE, cc * 2048 + 1024, 1024),
                              arb(GATE, cc * 2048 + 1024, 1024), TRH, OP.mult),
                 nd=True)
            traf_guard[0], traf_guard[1] = sc2_0, sc2_1
            sgi_guard[0], sgi_guard[1] = sc1_0, sc1_1
        pyl_done = p.cur("s_v")
        p.op("vector", vcopy(t["sum24"][:, 0:8], t["dcar"]))
        sm = p.op("vector", vcopy(t["sum24"][:, 8:16], t["hcar"]), inc="s_v")
        rb4 = emit_ag(1, 2, sm, False)
        cmb_m = emit_combine(rb4)
        for mk in cmb_m:
            p.wait("vector", mk)
        cmb_v = p.cur("s_v")
        p.wait("gpsimd", prev_cc_mm)
        for cc in range(CH):
            dma("gpsimd", arb(BB, cc * 1024, 1024),
                wB_outT[cc * 128:(cc + 1) * 128, :], sem="s_w")
        wo_mark = p.cur("s_w")
        p.wait("tensor", wo_mark)
        p.wait("vector", pyl_done)
        ob = Banks(p, [PF[0], PF[1], PF[2], PF[3]])
        prev_ng_mm = None
        for gi, (c0, w) in enumerate(NGS):
            p.wait("vector", prev_ng_mm)
            ysl = None
            for cc in range(CH):
                ysl = p.op("vector", stt(arb(BB, 8192 + cc * 512, 512),
                                         arb(XB, cc * 2048 + c0, w),
                                         t["hp"][:, cc:cc + 1],
                                         arb(GATE, cc * 2048 + c0, w),
                                         OP.mult, OP.add), inc="s_v", nd=cc > 0)
            p.wait("tensor", ysl)
            for jo in range(8):
                bi, ps, bmk = ob.get()
                p.wait("tensor", bmk)
                mmark = None
                for k in range(CH):
                    f = mm(ps[:, 0:w], arb(BB, k * 1024 + jo * 128, 128),
                           arb(BB, 8192 + k * 512, 512), k == 0, k == CH - 1)
                    if k == CH - 1:
                        mmark = p.op("tensor", f, inc="s_mm")
                    else:
                        p.op("tensor", f)
                p.wait("vector", mmark)
                ev = p.op("vector", tt(res_c(jo, c0, w), ps, res_c(jo, c0, w),
                                       OP.add), inc="s_v", nd=jo > 0)
                ob.set_mark(bi, ev)
            prev_ng_mm = p.cur("s_mm")
        return p.cur("s_v"), p.cur("s_mm")

    done_v, done_mm = p.cur("s_v"), None
    dbg_exit = False
    if phase_limit >= 1:
        r = attn_a()
        if r is None:
            dbg_exit = True
        else:
            done_v, done_mm = r
    if not dbg_exit and phase_limit >= 2:
        done_v, done_mm = ffn(done_v, done_mm)
    if not dbg_exit and phase_limit >= 3:
        done_v, done_mm = attn_b(done_v, done_mm)
    if not dbg_exit and phase_limit >= 4:
        done_v, done_mm = ffn(done_v, done_mm)

    if not dbg_exit:
        p.wait("sync", done_v)
        p.wait("sync", done_mm)
        stm = dma("sync", out_d.rearrange("(c p) q -> p c q", p=128),
                  RES.rearrange("p (c q) -> p c q", q=T), sem="s_st")
        p.wait("sync", stm)

    import contextlib
    with contextlib.ExitStack() as stack:
        for s in ["s_x", "s_w", "s_mm", "s_sc", "s_v", "s_g", "s_cc", "s_ag",
                  "s_st", "s_d0", "s_d1", "s_d2", "s_f1", "s_f2", "s_fo",
                  "s_f1b", "s_f2b", "s_p0", "s_p1", "s_b0", "s_b1"]:
            p.sems[s] = stack.enter_context(nc.semaphore(s))
        with nc.Block() as block:
            @block.sync
            def _(e):
                for f in p.st["sync"]:
                    f(e)

            @block.tensor
            def _(e):
                for f in p.st["tensor"]:
                    f(e)

            @block.vector
            def _(e):
                for f in p.st["vector"]:
                    f(e)

            @block.scalar
            def _(e):
                for f in p.st["scalar"]:
                    f(e)

            @block.gpsimd
            def _(e):
                for f in p.st["gpsimd"]:
                    f(e)
    return nc


def _prep_inputs(inp):
    import ml_dtypes
    f32 = np.float32
    bf = ml_dtypes.bfloat16

    def ct(a):
        return np.ascontiguousarray(np.asarray(a, f32))

    def chunks8(v):
        return np.ascontiguousarray(np.asarray(v, f32).reshape(8, 128).T)

    def conv_lay(w):
        return np.ascontiguousarray(np.asarray(w, f32)[:, 0, :]
                                    .reshape(8, 128, 4).transpose(1, 0, 2)
                                    .reshape(128, 32))

    shared = {
        "wA_inT": ct(np.asarray(inp["L0_attn_a_w_in"]).T),
        "wA_dT": ct(np.asarray(inp["L0_attn_a_w_delta"]).T),
        "wA_BT": ct(np.asarray(inp["L0_attn_a_w_B"]).T),
        "wA_CT": ct(np.asarray(inp["L0_attn_a_w_C"]).T),
        "wA_outT": ct(np.asarray(inp["L0_attn_a_w_out"]).T),
        "wB_inT": ct(np.asarray(inp["L0_attn_b_w_in"]).T),
        "wB_gT": ct(np.asarray(inp["L0_attn_b_w_gates"]).T),
        "wB_outT": ct(np.asarray(inp["L0_attn_b_w_out"]).T),
        "f_w1T": ct(np.asarray(inp["L0_ffn_w1"]).T),
        "f_w2T": ct(np.asarray(inp["L0_ffn_w2"]).T),
        "binA": np.ascontiguousarray(ct(inp["L0_attn_a_b_in"]).reshape(16, 128).T),
        "lng": chunks8(inp["L0_attn_a_ln_g"]),
        "lnb": chunks8(inp["L0_attn_a_ln_b"]),
        "wcA": conv_lay(inp["L0_attn_a_w_conv"]),
        "bcA": chunks8(inp["L0_attn_a_b_conv"]),
        "bdA": chunks8(inp["L0_attn_a_b_delta"]),
        "bBA": chunks8(inp["L0_attn_a_b_B"]),
        "bCA": chunks8(inp["L0_attn_a_b_C"]),
        "boA": chunks8(inp["L0_attn_a_b_out"]),
        "wcB": conv_lay(inp["L0_attn_b_w_conv"]),
        "bcB": chunks8(inp["L0_attn_b_b_conv"]),
        "bgB": np.ascontiguousarray(ct(inp["L0_attn_b_b_gates"]).reshape(16, 128).T),
        "fb": chunks8(inp["L0_attn_b_forget_base"]),
        "rT": np.ascontiguousarray(ct(inp["L0_ffn_router_w"]).T
                                   .reshape(8, 128, 16).transpose(1, 0, 2)
                                   .reshape(128, 128)).astype(bf),
        "rb": ct(inp["L0_ffn_router_b"]).reshape(16, 1),
        "fb1": np.ascontiguousarray(ct(inp["L0_ffn_b1"]).reshape(32, 128).T),
        "fb2": chunks8(inp["L0_ffn_b2"]),
        "eye16": np.eye(16, dtype=f32),
        "eye128f": np.eye(128, dtype=f32),
        "eye128b": np.eye(128, dtype=bf),
    }
    oh = np.zeros((16, 2048), dtype=bf)
    for b in range(16):
        oh[b, b * 128:(b + 1) * 128] = 1
    shared["oh16"] = oh
    selAH = np.zeros((16, 16), f32)
    for i in range(8):
        selAH[2 * i, i] = 1.0
        selAH[2 * i + 1, 8 + i] = 1.0
    shared["selAH"] = selAH

    x = np.asarray(inp["L0_attn_a_x"], f32)[0]
    cores = []
    for c in range(NCORES):
        hs = np.zeros((24, 3), f32)
        if c > 0:
            for r in range(3):
                hs[(c - 1) * 3 + r, r] = 1.0
        sp = np.zeros((128, 8), f32)
        sp[:, :c] = 1.0
        cores.append({
            "x_fm": np.ascontiguousarray(x[c * T:(c + 1) * T].T),
            "hsel": hs,
            "spre": sp,
        })
    return shared, cores


def kernel(**inputs):
    shared, cores = _prep_inputs(inputs)
    nc = build_nc(PHASE_LIMIT)
    in_maps = []
    for c in range(NCORES):
        m = dict(shared)
        m.update(cores[c])
        in_maps.append(m)
    res = run_bass_kernel_spmd(nc, in_maps, core_ids=list(range(NCORES)))
    out = np.empty((1, NCORES * T, D), np.float32)
    for c in range(NCORES):
        out[0, c * T:(c + 1) * T] = res.results[c]["out_fm"].T
    return out
